# version 04
# baseline (speedup 1.0000x reference)
import numpy as np
from contextlib import ExitStack
import concourse.bass as bass
import concourse.mybir as mybir
from concourse.bass_utils import run_bass_kernel_spmd

F32 = mybir.dt.float32
BF16 = mybir.dt.bfloat16
I32 = mybir.dt.int32
AF = mybir.ActivationFunctionType
ALU = mybir.AluOpType
AX = mybir.AxisListType.X

D = 2048
H = 16
HD = 128
C = 128
FF = 8192
NKD = D // 128
NEB = D // 512
NFB = FF // 512
NKF = FF // 128
EPS = 1e-5
HUGE_DELTA = 1e7


class Op:
    __slots__ = ("eng", "fn", "reads", "writes", "dma", "waits", "need_inc",
                 "need_drain", "semval", "idx", "scope")

    def __init__(self, eng, fn, reads, writes, dma, scope=""):
        self.eng, self.fn = eng, fn
        self.reads, self.writes, self.dma = reads, writes, dma
        self.waits = []
        self.need_inc = False
        self.need_drain = False
        self.semval = None
        self.idx = None
        self.scope = scope


class Barrier:
    pass


class Prog:
    ENGINES = ("sync", "scalar", "vector", "tensor", "gpsimd")

    def __init__(self, nc, stack):
        self.nc = nc
        self.stack = stack
        self.ops = []
        self.last_w = {}
        self.readers = {}
        self.n_bar = 0

    def scope(self, name):
        self.cur_scope = name

    def do(self, eng, fn, reads=(), writes=(), dma=False):
        op = Op(eng, fn, tuple(reads), tuple(writes), dma, getattr(self, "cur_scope", ""))
        deps = []
        for r in op.reads:
            p = self.last_w.get(r)
            if p is not None:
                deps.append(p)
        for w in op.writes:
            p = self.last_w.get(w)
            if p is not None:
                deps.append(p)
            deps.extend(self.readers.get(w, ()))
        for p in deps:
            if p is op:
                continue
            if p.eng == eng:
                if eng != "tensor" and not (op.dma and p.dma):
                    op.need_drain = True
            else:
                p.need_inc = True
                op.waits.append(p)
        for r in op.reads:
            self.readers.setdefault(r, []).append(op)
        for w in op.writes:
            self.last_w[w] = op
            self.readers[w] = []
        self.ops.append(op)
        return op

    def barrier(self):
        self.ops.append(Barrier())
        self.last_w = {}
        self.readers = {}

    def finalize_and_replay(self, block):
        sems = {}
        for e in self.ENGINES:
            sems[e] = self.stack.enter_context(self.nc.semaphore("sem_" + e))
        barsem = self.stack.enter_context(self.nc.semaphore("sem_bar"))

        n_ops = len(self.ops)
        last_before_bar = {}
        for i, op in enumerate(self.ops):
            if isinstance(op, Barrier):
                for e, p in last_before_bar.items():
                    p.need_inc = True
                last_before_bar = {}
            else:
                last_before_bar[op.eng] = op
        for op in self.ops:
            if not isinstance(op, Barrier) and op.dma:
                op.need_inc = True
        counts = {e: 0 for e in self.ENGINES}
        bar_marks = []
        for op in self.ops:
            if isinstance(op, Barrier):
                bar_marks.append(dict(counts))
                continue
            if op.need_inc:
                counts[op.eng] += 16 if op.dma else 1
                op.semval = counts[op.eng]

        per_engine = {e: [] for e in self.ENGINES}
        bar_i = [0]
        bar_total = [0]
        pending_waits = {e: {} for e in self.ENGINES}
        bar_floor = {e: {} for e in self.ENGINES}

        bar_count = 0
        floors = {e: {s: 0 for s in self.ENGINES} for e in self.ENGINES}
        for op in self.ops:
            if isinstance(op, Barrier):
                bar_count += 1
                snap = bar_marks[bar_count - 1]
                target = bar_count * len(self.ENGINES)
                for e in self.ENGINES:
                    def emit_bar(eng_obj, snap=snap, target=target, e=e):
                        for s in self.ENGINES:
                            if s != e and snap[s] > 0:
                                eng_obj.wait_ge(sems[s], snap[s])
                        eng_obj.nop().then_inc(barsem, 1)
                        eng_obj.wait_ge(barsem, target)
                    per_engine[e].append(("bar", emit_bar))
                    for s in self.ENGINES:
                        floors[e][s] = max(floors[e][s], snap[s])
                continue
            waits = {}
            for p in op.waits:
                if p.semval is None:
                    continue
                if p.semval <= floors[op.eng][p.eng]:
                    continue
                waits[p.eng] = max(waits.get(p.eng, 0), p.semval)

            def emit(eng_obj, op=op, waits=waits):
                for s, v in waits.items():
                    eng_obj.wait_ge(sems[s], v)
                if op.need_drain:
                    eng_obj.drain()
                instr = op.fn(eng_obj)
                if op.need_inc:
                    assert instr is not None, "op needs inc but returned no instruction"
                    instr.then_inc(sems[op.eng], 16 if op.dma else 1)
            per_engine[op.eng].append((op.scope, emit))

        nc = self.nc
        for engname in self.ENGINES:
            ops = per_engine[engname]
            def section(e, ops=ops):
                cur = [None, None]
                for sc, f in ops:
                    if sc != cur[0]:
                        if cur[0]:
                            nc.leave_named_scope(cur[0], cur[1], False)
                        if sc:
                            cur[1], _ = nc.enter_named_scope(sc, False)
                        cur[0] = sc
                    f(e)
                if cur[0]:
                    nc.leave_named_scope(cur[0], cur[1], False)
            getattr(block, engname)(section)


DIAG_BARRIER = False
SKIP_PREP = False
SKIP_LATE = False
SKIP_A2_BODY = False
A2_ONLY_WLOADS = False


def build(n_chunks=32, n_cores=8, taps=(), mm="f32"):
    nc = bass.Bass("TRN2", target_bir_lowering=False)
    T = n_chunks * C

    x_in = nc.dram_tensor("x", [T, D], F32, kind="ExternalInput").ap()
    ld_in = nc.dram_tensor("log_decay", [1, H], F32, kind="ExternalInput").ap()
    Wq_in = nc.dram_tensor("Wq", [D, D], F32, kind="ExternalInput").ap()
    Wk_in = nc.dram_tensor("Wk", [D, D], F32, kind="ExternalInput").ap()
    Wv_in = nc.dram_tensor("Wv", [D, D], F32, kind="ExternalInput").ap()
    Wo_in = nc.dram_tensor("Wo", [D, D], F32, kind="ExternalInput").ap()
    Wg1_in = nc.dram_tensor("Wg1", [D, HD], F32, kind="ExternalInput").ap()
    Wg2_in = nc.dram_tensor("Wg2", [HD, D], F32, kind="ExternalInput").ap()
    nw_in = nc.dram_tensor("norm_w", [1, D], F32, kind="ExternalInput").ap()
    w1_in = nc.dram_tensor("w1", [FF, D], F32, kind="ExternalInput").ap()
    b1_in = nc.dram_tensor("b1", [1, FF], F32, kind="ExternalInput").ap()
    w2_in = nc.dram_tensor("w2", [D, FF], F32, kind="ExternalInput").ap()
    b2_in = nc.dram_tensor("b2", [1, D], F32, kind="ExternalInput").ap()
    sdel_in = nc.dram_tensor("span_deltas", [1, n_cores], F32, kind="ExternalInput").ap()
    y_out = nc.dram_tensor("y", [T, D], F32, kind="ExternalOutput").ap()

    tap_outs = {}

    def tap(name, shape, dtype=F32):
        if name in taps:
            tap_outs[name] = nc.dram_tensor("tap_" + name, shape, dtype,
                                            kind="ExternalOutput").ap()
            return tap_outs[name]
        return None

    xnT_f_d = nc.dram_tensor("xnT_f_d", [n_chunks * 128, D], F32).ap()
    kT_d = nc.dram_tensor("kT_d", [n_chunks * 128, D], F32).ap()
    v_d = nc.dram_tensor("v_d", [n_chunks * 128, D], F32).ap()
    kv_d = nc.dram_tensor("kv_d", [n_chunks * 128, H * HD], F32).ap()
    ogT_f_d = nc.dram_tensor("ogT_f_d", [n_chunks * 128, D], F32).ap()
    w1q_d = nc.dram_tensor("w1q_d", [FF, D], BF16).ap()
    w2q_d = nc.dram_tensor("w2q_d", [D, FF], BF16).ap()
    h_d = nc.dram_tensor("h_d", [T, FF], F32).ap()
    g_d = nc.dram_tensor("g_d", [T, D], F32).ap()
    cc_in_d = nc.dram_tensor("cc_in_d", [128, H * HD], F32).ap()
    cc_out_d = nc.dram_tensor("cc_out_d", [n_cores * 128, H * HD], F32).ap()

    stack = ExitStack()
    P = Prog(nc, stack)
    cur_stack = [stack]

    def sb(name, shape, dtype=F32):
        return cur_stack[-1].enter_context(nc.sbuf_tensor(name, shape, dtype))

    def phase_open():
        cur_stack.append(ExitStack())

    def phase_close():
        P.barrier()
        cur_stack.pop().close()

    with stack:
        ident = sb("ident", [128, 128], F32)
        ldb = sb("ldb", [128, H], F32)
        DTT = sb("DTT", [128, H * 128], F32)
        kdecfull = sb("kdecfull", [128, D], F32)
        qdecTfull = sb("qdecTfull", [128, D], F32)
        cdbc = sb("cdbc", [128, H], F32)
        cdfull = sb("cdfull", [128, D], F32)
        coefsh = sb("coefsh", [128, n_cores * H], F32)
        epsv = sb("epsv", [128, 1], F32)
        sdel_sb = sb("sdel_sb", [128, n_cores], F32)
        ones_row = sb("ones_row", [1, 512], F32)
        onesc = sb("onesc", [128, 128], F32)
        mu_sb = sb("mu_sb", [128, max(n_chunks, 1)], F32)
        rstd_sb = sb("rstd_sb", [128, max(n_chunks, 1)], F32)
        S_sb = sb("S_sb", [128, H * HD], F32)
        swb = sb("swb", [128, 4], F32)
        a1_sb = sb("a1_sb", [128, max(n_chunks, 1)], F32)
        a2_sb = sb("a2_sb", [128, max(n_chunks, 1)], F32)
        aq2_sb = sb("aq2_sb", [128, max(n_chunks, 1)], F32)
        ub2_sb = sb("ub2_sb", [128, max(n_chunks, 1)], F32)
        identb = sb("identb", [128, 128], BF16)

        PSUM = [stack.enter_context(nc.psum_tensor("pb%d" % i, [128, 512], F32))
                for i in range(8)]
        PK = ["psum%d" % i for i in range(8)]

        P.scope("consts")
        phase_open()
        P.do("sync", lambda e: e.dma_start(out=ldb[:1, :], in_=ld_in[:, :]),
             writes=["ldb_row"], dma=True)
        P.do("sync", lambda e: e.dma_start(out=sdel_sb[:1, :], in_=sdel_in[:, :]),
             writes=["sdel_row"], dma=True)

        it_a = sb("it_a", [128, 128], F32)
        relu_d = sb("relu_d", [128, 128], F32)
        P.do("gpsimd", lambda e: e.iota(it_a[:, :], pattern=[[1, 128]], channel_multiplier=-1,
                                        allow_small_or_imprecise_dtypes=True), writes=["it_a"])
        P.do("vector", lambda e: e.tensor_single_scalar(ident[:, :], it_a[:, :], 0.0, ALU.is_equal),
             reads=["it_a"], writes=["ident"])
        P.do("vector", lambda e: e.tensor_copy(identb[:, :], ident[:, :]),
             reads=["ident"], writes=["identb"])
        P.do("vector", lambda e: e.memset(ones_row[:, :], 1.0), writes=["ones_row"])
        P.do("vector", lambda e: e.memset(onesc[:, :], 1.0), writes=["onesc"])
        P.do("vector", lambda e: e.memset(epsv[:, :], EPS), writes=["epsv"])

        P.do("tensor", lambda e: e.matmul(PSUM[0][:, :H], ones_row[:1, :128], ldb[:1, :H],
                                          start=True, stop=True),
             reads=["ones_row", "ldb_row"], writes=[PK[0]])
        P.do("vector", lambda e: e.tensor_copy(ldb[:, :], PSUM[0][:, :H]),
             reads=[PK[0]], writes=["ldb"])
        P.do("tensor", lambda e: e.matmul(PSUM[0][:, :n_cores], ones_row[:1, :128],
                                          sdel_sb[:1, :n_cores], start=True, stop=True),
             reads=["ones_row", "sdel_row"], writes=[PK[0]])
        P.do("vector", lambda e: e.tensor_copy(sdel_sb[:, :], PSUM[0][:, :n_cores]),
             reads=[PK[0]], writes=["sdel"])

        P.do("vector", lambda e: e.tensor_single_scalar(relu_d[:, :], it_a[:, :], 0.0, ALU.max),
             reads=["it_a"], writes=["relu_d"])
        P.do("vector", lambda e: e.tensor_single_scalar(it_a[:, :], it_a[:, :], 0.0, ALU.is_ge),
             reads=["it_a"], writes=["it_a"])
        for h in range(H):
            P.do("scalar", lambda e, h=h: e.activation(DTT[:, 128 * h:128 * h + 128], relu_d[:, :],
                                                       AF.Exp, bias=0.0, scale=ldb[:, h:h + 1]),
                 reads=["relu_d", "ldb"], writes=["DTT"])
        for h in range(H):
            P.do("vector", lambda e, h=h: e.tensor_tensor(DTT[:, 128 * h:128 * h + 128],
                                                          DTT[:, 128 * h:128 * h + 128],
                                                          it_a[:, :], ALU.mult),
                 reads=["DTT", "it_a"], writes=["DTT"])

        iot_col = sb("iot_col", [128, 1], F32)
        t127 = sb("t127", [128, 1], F32)
        kdec1 = sb("kdec1", [128, H], F32)
        P.do("gpsimd", lambda e: e.iota(iot_col[:, :], pattern=[[0, 1]], channel_multiplier=1,
                                        allow_small_or_imprecise_dtypes=True), writes=["iot_col"])
        P.do("vector", lambda e: e.tensor_scalar(t127[:, :], iot_col[:, :], -1.0, 127.0,
                                                 ALU.mult, ALU.add),
             reads=["iot_col"], writes=["t127"])
        for h in range(H):
            P.do("scalar", lambda e, h=h: e.activation(kdec1[:, h:h + 1], t127[:, :], AF.Exp,
                                                       bias=0.0, scale=ldb[:, h:h + 1]),
                 reads=["t127", "ldb"], writes=["kdec1"])
        for h in range(H):
            P.do("vector", lambda e, h=h: e.tensor_single_scalar(
                kdecfull[:, 128 * h:128 * h + 128], onesc[:, :], kdec1[:, h:h + 1], ALU.mult),
                reads=["onesc", "kdec1"], writes=["kdecfull"])
        iot_row = sb("iot_row", [128, 128], F32)
        P.do("gpsimd", lambda e: e.iota(iot_row[:, :], pattern=[[1, 128]], channel_multiplier=0,
                                        allow_small_or_imprecise_dtypes=True), writes=["iot_row"])
        P.do("vector", lambda e: e.tensor_single_scalar(iot_row[:, :], iot_row[:, :], 1.0, ALU.add),
             reads=["iot_row"], writes=["iot_row"])
        for h in range(H):
            P.do("scalar", lambda e, h=h: e.activation(qdecTfull[:, 128 * h:128 * h + 128],
                                                       iot_row[:, :], AF.Exp, bias=0.0,
                                                       scale=ldb[:, h:h + 1]),
                 reads=["iot_row", "ldb"], writes=["qdecT"])
        c128 = sb("c128", [128, 1], F32)
        P.do("vector", lambda e: e.memset(c128[:, :], 128.0), writes=["c128"])
        for h in range(H):
            P.do("scalar", lambda e, h=h: e.activation(cdbc[:, h:h + 1], c128[:, :], AF.Exp,
                                                       bias=0.0, scale=ldb[:, h:h + 1]),
                 reads=["c128", "ldb"], writes=["cdbc"])
        for h in range(H):
            P.do("vector", lambda e, h=h: e.tensor_single_scalar(
                cdfull[:, 128 * h:128 * h + 128], onesc[:, :], cdbc[:, h:h + 1], ALU.mult),
                reads=["onesc", "cdbc"], writes=["cdfull"])
        for k in range(n_cores):
            for h in range(H):
                P.do("scalar", lambda e, k=k, h=h: e.activation(
                    coefsh[:, k * H + h:k * H + h + 1], sdel_sb[:, k:k + 1], AF.Exp,
                    bias=0.0, scale=ldb[:, h:h + 1]),
                    reads=["sdel", "ldb"], writes=["coefsh"])

        P.do("vector", lambda e: e.memset(S_sb[:, :], 0.0), writes=["S"])

        for nm, ap_ in (("DTT", DTT), ("qdecT", qdecTfull), ("kdec", kdecfull),
                        ("coefsh", coefsh), ("cdbc", cdbc)):
            if tp := tap(nm, list(ap_.shape)):
                P.do("gpsimd", lambda e, tp=tp, ap_=ap_: e.dma_start(out=tp[:, :], in_=ap_[:, :]),
                     reads=[nm if nm != "cdbc" else "cdbc"], writes=["tapst"], dma=True)

        phase_close()

        P.scope("wprep")
        if SKIP_PREP:
            class _SkipCtx:
                pass

        phase_open()
        wld = [sb("wld%d" % i, [128, FF], F32) for i in range(2)]
        wq_t = sb("wq_t", [128, FF], F32)
        wq_a = sb("wq_a", [128, FF], F32)
        wq_o = sb("wq_o", [128, FF], BF16)
        racc = sb("racc", [128, 1], F32)
        rtile = sb("rtile", [128, 1], F32)
        sc1 = sb("sc1", [1, 4], F32)

        for wi, (w_in, wq_d, nrow) in enumerate(() if SKIP_PREP else ((w1_in, w1q_d, FF), (w2_in, w2q_d, D))):
            ntile = nrow // 128
            ncol = D if wi == 0 else FF
            P.do("vector", lambda e: e.memset(racc[:, :], 0.0), writes=["racc"])
            for t in range(ntile):
                buf = wld[t % 2]
                bk = "wld%d" % (t % 2)
                P.do("sync", lambda e, t=t, buf=buf, w_in=w_in, ncol=ncol: e.dma_start(
                    out=buf[:, :ncol], in_=w_in[128 * t:128 * t + 128, :]),
                    writes=[bk], dma=True)
                P.do("vector", lambda e, buf=buf, ncol=ncol: e.reduce_sum(
                    rtile[:, :], buf[:, :ncol], axis=AX, apply_absolute_value=True),
                    reads=[bk], writes=["rtile"])
                P.do("vector", lambda e: e.tensor_tensor(racc[:, :], racc[:, :], rtile[:, :],
                                                         ALU.add),
                     reads=["racc", "rtile"], writes=["racc"])
            P.do("tensor", lambda e: e.matmul(PSUM[0][:1, :1], racc[:, :1], onesc[:, :1],
                                              start=True, stop=True),
                 reads=["racc", "onesc"], writes=[PK[0]])
            P.do("vector", lambda e, ncol=ncol, nrow=nrow: e.tensor_scalar(
                sc1[:1, 0:1], PSUM[0][:1, :1], 1.0 / (nrow * ncol), 1e-8, ALU.mult, ALU.max),
                reads=[PK[0]], writes=["sc1"])
            P.do("vector", lambda e: e.reciprocal(sc1[:1, 1:2], sc1[:1, 0:1]),
                 reads=["sc1"], writes=["sc1"])
            P.do("vector", lambda e: e.tensor_copy(sc1[:1, 2:3], sc1[:1, 0:1]),
                 reads=["sc1"], writes=["sc1"])
            P.do("vector", lambda e: e.tensor_copy(sc1[:1, 0:1], sc1[:1, 1:2]),
                 reads=["sc1"], writes=["sc1"])
            P.do("vector", lambda e: e.tensor_copy(sc1[:1, 1:2], sc1[:1, 2:3]),
                 reads=["sc1"], writes=["sc1"])
            P.do("tensor", lambda e: e.matmul(PSUM[0][:, 0:2], ones_row[:1, :128], sc1[:1, 0:2],
                                              start=True, stop=True),
                 reads=["ones_row", "sc1"], writes=[PK[0]])
            P.do("vector", lambda e, wi=wi: e.tensor_copy(swb[:, 2 * wi:2 * wi + 2],
                                                          PSUM[0][:, 0:2]),
                 reads=[PK[0]], writes=["swb"])
            for t in range(ntile):
                buf = wld[t % 2]
                bk = "wld%d" % (t % 2)
                P.do("sync", lambda e, t=t, buf=buf, w_in=w_in, ncol=ncol: e.dma_start(
                    out=buf[:, :ncol], in_=w_in[128 * t:128 * t + 128, :]),
                    writes=[bk], dma=True)
                P.do("vector", lambda e, buf=buf, wi=wi, ncol=ncol: e.tensor_single_scalar(
                    wq_t[:, :ncol], buf[:, :ncol], swb[:, 2 * wi:2 * wi + 1], ALU.mult),
                    reads=[bk, "swb"], writes=["wq_t"])
                P.do("vector", lambda e, ncol=ncol: e.tensor_single_scalar(
                    wq_a[:, :ncol], wq_t[:, :ncol], 0.5, ALU.is_gt),
                    reads=["wq_t"], writes=["wq_a"])
                P.do("vector", lambda e, ncol=ncol: e.scalar_tensor_tensor(
                    wq_o[:, :ncol], wq_t[:, :ncol], -0.5, wq_a[:, :ncol],
                    ALU.is_lt, ALU.subtract),
                    reads=["wq_t", "wq_a"], writes=["wq_o"])
                P.do("gpsimd", lambda e, t=t, wq_d=wq_d, ncol=ncol: e.dma_start(
                    out=wq_d[128 * t:128 * t + 128, :], in_=wq_o[:, :ncol]),
                    reads=["wq_o"], writes=["wq_d%d" % wi], dma=True)
        if tp := tap("w1q", [256, D], BF16):
            P.do("sync", lambda e, tp=tp: e.dma_start(out=tp[:, :], in_=w1q_d[0:256, :]),
                 reads=["wq_d0"], writes=["tapst"], dma=True)
        phase_close()

        phase_open()
        P.scope("a1")
        xa = sb("xa", [128, D], F32)
        xna = sb("xna", [128, D], F32)
        red8 = sb("red8", [128, 8], F32)
        xnTf = sb("xnTf", [128, D], F32)
        sq_scr = xnTf

        for j in range(n_chunks):
            P.do("sync", lambda e, j=j: e.dma_start(out=xa[:, :], in_=x_in[C * j:C * j + C, :]),
                 writes=["xa"], dma=True)
            P.do("vector", lambda e: e.reduce_sum(red8[:, 0:1], xa[:, :], axis=AX),
                 reads=["xa"], writes=["red_s"])
            P.do("scalar", lambda e: e.activation(sq_scr[:, :], xa[:, :], AF.Square,
                                                  accum_out=red8[:, 1:2]),
                 reads=["xa"], writes=["red_q", "xnTf"])
            P.do("vector", lambda e, j=j: e.tensor_single_scalar(mu_sb[:, j:j + 1], red8[:, 0:1],
                                                                 1.0 / D, ALU.mult),
                 reads=["red_s"], writes=["mu"])
            P.do("vector", lambda e, j=j: e.tensor_tensor(red8[:, 2:3], mu_sb[:, j:j + 1],
                                                          mu_sb[:, j:j + 1], ALU.mult),
                 reads=["mu"], writes=["red_m2"])
            P.do("vector", lambda e: e.tensor_scalar(red8[:, 3:4], red8[:, 1:2], 1.0 / D, None,
                                                     ALU.mult, ALU.bypass),
                 reads=["red_q"], writes=["red_v"])
            P.do("vector", lambda e: e.tensor_tensor(red8[:, 3:4], red8[:, 3:4], red8[:, 2:3],
                                                     ALU.subtract),
                 reads=["red_v", "red_m2"], writes=["red_v"])
            P.do("scalar", lambda e: e.activation(red8[:, 4:5], red8[:, 3:4], AF.Sqrt,
                                                  bias=epsv[:, :], scale=1.0),
                 reads=["red_v", "epsv"], writes=["red_sd"])
            P.do("vector", lambda e, j=j: e.reciprocal(rstd_sb[:, j:j + 1], red8[:, 4:5]),
                 reads=["red_sd"], writes=["rstd"])
            P.do("vector", lambda e, j=j: e.tensor_scalar(xna[:, :], xa[:, :], mu_sb[:, j:j + 1],
                                                          rstd_sb[:, j:j + 1], ALU.subtract,
                                                          ALU.mult),
                 reads=["xa", "mu", "rstd"], writes=["xna"])
            for b in range(NKD):
                P.do("tensor", lambda e, b=b: e.transpose(
                    PSUM[b // 4][:, 128 * (b % 4):128 * (b % 4) + 128],
                    xna[:, 128 * b:128 * b + 128], ident[:, :]),
                    reads=["xna", "ident"], writes=[PK[b // 4]])
            for q4 in range(4):
                P.do("vector", lambda e, q4=q4: e.tensor_copy(xnTf[:, 512 * q4:512 * q4 + 512],
                                                              PSUM[q4][:, :]),
                     reads=[PK[q4]], writes=["xnTf"])
            P.do("gpsimd", lambda e, j=j: e.dma_start(out=xnT_f_d[128 * j:128 * j + 128, :],
                                                      in_=xnTf[:, :]),
                 reads=["xnTf"], writes=["xnT_d"], dma=True)
            if j == 0 and (tp := tap("xn0", [128, D])):
                P.do("gpsimd", lambda e, tp=tp: e.dma_start(out=tp[:, :], in_=xna[:, :]),
                     reads=["xna"], writes=["tapst"], dma=True)
            if j == 0 and (tp := tap("xnT0", [128, D])):
                P.do("gpsimd", lambda e, tp=tp: e.dma_start(out=tp[:, :], in_=xnTf[:, :]),
                     reads=["xnTf"], writes=["tapst"], dma=True)

        P.barrier()

        P.scope("a2")
        TILE_A = min(8, n_chunks)
        n_tiles_A = n_chunks // TILE_A
        wkbuf = sb("wkbuf", [128, NKD * 512], F32)
        wvbuf = sb("wvbuf", [128, NKD * 512], F32)
        xnT_tile = sb("xnT_tile", [128, TILE_A * D], F32)
        kstage = sb("kstage", [128, 512], F32)
        ktil = sb("ktil", [128, 512], F32)
        vstage = sb("vstage", [128, 512], F32)
        kTstage = sb("kTstage", [128, 512], F32)
        kvstage = sb("kvstage", [128, 512], F32)

        def wslice_load(wbuf, wkey, w_in, eb, ew=512):
            for kk in range(NKD):
                P.do("sync", lambda e, kk=kk: e.dma_start(
                    out=wbuf[:, ew * kk:ew * kk + ew],
                    in_=w_in[128 * kk:128 * kk + 128, ew * eb:ew * eb + ew]),
                    writes=[wkey], dma=True)

        for ta in range(n_tiles_A):
            for j in range(TILE_A):
                cj = ta * TILE_A + j
                P.do("sync", lambda e, j=j, cj=cj: e.dma_start(
                    out=xnT_tile[:, D * j:D * j + D], in_=xnT_f_d[128 * cj:128 * cj + 128, :]),
                    reads=["xnT_d"], writes=["xnT_tile"], dma=True)
            if ta == 0 and (tp := tap("xnT_a2", [TILE_A * 128, D])):
                for jj in range(TILE_A):
                    P.do("gpsimd", lambda e, jj=jj, tp=tp: e.dma_start(
                        out=tp[128 * jj:128 * jj + 128, :],
                        in_=xnT_tile[:, D * jj:D * jj + D]),
                        reads=["xnT_tile"], writes=["tapst"], dma=True)
            for eb in range(0) if SKIP_A2_BODY else range(NEB):
                wslice_load(wkbuf, "wkbuf", Wk_in, eb)
                wslice_load(wvbuf, "wvbuf", Wv_in, eb)
                for j in range(0) if A2_ONLY_WLOADS else range(TILE_A):
                    cj = ta * TILE_A + j

                    def mk_mms(e, j=j, wbuf=wkbuf, bank=4):
                        for kk in range(NKD):
                            inst = e.matmul(PSUM[bank][:, :],
                                            xnT_tile[:, D * j + 128 * kk:D * j + 128 * kk + 128],
                                            wbuf[:, 512 * kk:512 * kk + 512],
                                            start=(kk == 0), stop=(kk == NKD - 1))
                        return inst
                    P.do("tensor", mk_mms, reads=["xnT_tile", "wkbuf"], writes=[PK[4]])
                    P.do("tensor", lambda e, j=j: mk_mms(e, j, wvbuf, 5),
                         reads=["xnT_tile", "wvbuf"], writes=[PK[5]])
                    P.do("scalar", lambda e: e.activation(kstage[:, :], PSUM[4][:, :], AF.Silu),
                         reads=[PK[4]], writes=["kstage"])
                    P.do("scalar", lambda e: e.activation(vstage[:, :], PSUM[5][:, :], AF.Silu),
                         reads=[PK[5]], writes=["vstage"])
                    P.do("gpsimd", lambda e, cj=cj, eb=eb: e.dma_start(
                        out=v_d[128 * cj:128 * cj + 128, 512 * eb:512 * eb + 512],
                        in_=vstage[:, :]),
                        reads=["vstage"], writes=["v_d"], dma=True)
                    P.do("vector", lambda e, eb=eb: e.tensor_tensor(
                        ktil[:, :], kstage[:, :], kdecfull[:, 512 * eb:512 * eb + 512], ALU.mult),
                        reads=["kstage", "kdecfull"], writes=["ktil"])
                    for hh in range(4):
                        P.do("tensor", lambda e, hh=hh: e.transpose(
                            PSUM[6][:, 128 * hh:128 * hh + 128],
                            kstage[:, 128 * hh:128 * hh + 128], ident[:, :]),
                            reads=["kstage", "ident"], writes=[PK[6]])
                    P.do("vector", lambda e: e.tensor_copy(kTstage[:, :], PSUM[6][:, :]),
                         reads=[PK[6]], writes=["kTstage"])
                    P.do("gpsimd", lambda e, cj=cj, eb=eb: e.dma_start(
                        out=kT_d[128 * cj:128 * cj + 128, 512 * eb:512 * eb + 512],
                        in_=kTstage[:, :]),
                        reads=["kTstage"], writes=["kT_d"], dma=True)

                    def kv_mms(e):
                        for hh in range(4):
                            inst = e.matmul(PSUM[7][:, 128 * hh:128 * hh + 128],
                                            ktil[:, 128 * hh:128 * hh + 128],
                                            vstage[:, 128 * hh:128 * hh + 128],
                                            start=True, stop=True)
                        return inst
                    P.do("tensor", kv_mms, reads=["ktil", "vstage"], writes=[PK[7]])
                    P.do("vector", lambda e, eb=eb: e.tensor_tensor(
                        S_sb[:, 512 * eb:512 * eb + 512], S_sb[:, 512 * eb:512 * eb + 512],
                        cdfull[:, 512 * eb:512 * eb + 512], ALU.mult),
                        reads=["S", "cdfull"], writes=["S"])
                    P.do("vector", lambda e, eb=eb: e.tensor_tensor(
                        S_sb[:, 512 * eb:512 * eb + 512], S_sb[:, 512 * eb:512 * eb + 512],
                        PSUM[7][:, :], ALU.add),
                        reads=["S", PK[7]], writes=["S"])
                    P.do("vector", lambda e: e.tensor_copy(kvstage[:, :], PSUM[7][:, :]),
                         reads=[PK[7]], writes=["kvstage"])
                    P.do("gpsimd", lambda e, cj=cj, eb=eb: e.dma_start(
                        out=kv_d[128 * cj:128 * cj + 128, 512 * eb:512 * eb + 512],
                        in_=kvstage[:, :]),
                        reads=["kvstage"], writes=["kv_d"], dma=True)
                    if DIAG_BARRIER:
                        P.barrier()

        if tp := tap("S_A", [128, H * HD]):
            P.do("gpsimd", lambda e, tp=tp: e.dma_start(out=tp[:, :], in_=S_sb[:, :]),
                 reads=["S"], writes=["tapst"], dma=True)
        if tp := tap("kT0", [128, D]):
            P.do("sync", lambda e, tp=tp: e.dma_start(out=tp[:, :], in_=kT_d[0:128, :]),
                 reads=["kT_d"], writes=["tapst"], dma=True)
        if tp := tap("v0", [128, D]):
            P.do("sync", lambda e, tp=tp: e.dma_start(out=tp[:, :], in_=v_d[0:128, :]),
                 reads=["v_d"], writes=["tapst"], dma=True)
        if tp := tap("kT_all", [n_chunks * 128, D]):
            P.do("sync", lambda e, tp=tp: e.dma_start(out=tp[:, :], in_=kT_d[:, :]),
                 reads=["kT_d"], writes=["tapst"], dma=True)
        if tp := tap("v_all", [n_chunks * 128, D]):
            P.do("sync", lambda e, tp=tp: e.dma_start(out=tp[:, :], in_=v_d[:, :]),
                 reads=["v_d"], writes=["tapst"], dma=True)
        if tp := tap("kv_all", [n_chunks * 128, H * HD]):
            P.do("sync", lambda e, tp=tp: e.dma_start(out=tp[:, :], in_=kv_d[:, :]),
                 reads=["kv_d"], writes=["tapst"], dma=True)
        if tp := tap("kv0", [128, H * HD]):
            P.do("sync", lambda e, tp=tp: e.dma_start(out=tp[:, :], in_=kv_d[0:128, :]),
                 reads=["kv_d"], writes=["tapst"], dma=True)

        phase_close()
        P.scope("coll")
        phase_open()
        P.do("gpsimd", lambda e: e.dma_start(out=cc_in_d[:, :], in_=S_sb[:, :]),
             reads=["S"], writes=["cc_in"], dma=True)
        P.do("gpsimd", lambda e: e.collective_compute(
            "AllGather", ALU.bypass, replica_groups=[list(range(n_cores))],
            ins=[cc_in_d[:, :]], outs=[cc_out_d[:, :]]),
            reads=["cc_in"], writes=["cc_out"])
        P.do("vector", lambda e: e.memset(S_sb[:, :], 0.0), reads=["S"], writes=["S"])
        Tk_buf = sb("Tk_buf", [128, H * HD], F32)
        for kcore in range(n_cores):
            P.do("sync", lambda e, kcore=kcore: e.dma_start(
                out=Tk_buf[:, :], in_=cc_out_d[128 * kcore:128 * kcore + 128, :]),
                reads=["cc_out"], writes=["Tk_buf"], dma=True)
            for h in range(H):
                P.do("vector", lambda e, kcore=kcore, h=h: e.scalar_tensor_tensor(
                    S_sb[:, 128 * h:128 * h + 128], Tk_buf[:, 128 * h:128 * h + 128],
                    coefsh[:, kcore * H + h:kcore * H + h + 1],
                    S_sb[:, 128 * h:128 * h + 128], ALU.mult, ALU.add),
                    reads=["Tk_buf", "coefsh", "S"], writes=["S"])
        if tp := tap("S_start", [128, H * HD]):
            P.do("gpsimd", lambda e, tp=tp: e.dma_start(out=tp[:, :], in_=S_sb[:, :]),
                 reads=["S"], writes=["tapst"], dma=True)

        phase_close()

        P.scope("p1")
        phase_open()
        TILE_1 = min(8, n_chunks)
        n_tiles_1 = n_chunks // TILE_1
        xnT_tile1 = sb("xnT_tile1", [128, TILE_1 * D], F32)
        wqbuf = sb("wqbuf", [128, NKD * 512], F32)
        wg1buf = sb("wg1buf", [128, NKD * 128], F32)
        wg2buf = sb("wg2buf", [128, D], F32)
        g1tile = sb("g1tile", [128, TILE_1 * 128], F32)
        qstage2 = [sb("qstage%d" % i, [128, 512], F32) for i in range(2)]
        qTs2 = [sb("qTs%d" % i, [128, 512], F32) for i in range(2)]
        qts2 = [sb("qts%d" % i, [128, 512], F32) for i in range(2)]
        kTr2 = [sb("kTr%d" % i, [128, 512], F32) for i in range(2)]
        vr2 = [sb("vr%d" % i, [128, 512], F32) for i in range(2)]
        kvr2 = [sb("kvr%d" % i, [128, 512], F32) for i in range(2)]
        scsb2 = [sb("scsb%d" % i, [128, 512], F32) for i in range(2)]
        sgst2 = [sb("sgst%d" % i, [128, 512], F32) for i in range(2)]
        ogst2 = [sb("ogst%d" % i, [128, 512], F32) for i in range(2)]
        sqh_b = sb("sqh", [128, 512], F32)
        ssq42 = [sb("ssq4%d" % i, [128, 4], F32) for i in range(2)]
        rinv42 = [sb("rinv4%d" % i, [128, 4], F32) for i in range(2)]
        ogTst2 = [sb("ogTst%d" % i, [128, 512], F32) for i in range(2)]
        g1s = sb("g1s", [128, 128], F32)

        for kk in range(NKD):
            P.do("sync", lambda e, kk=kk: e.dma_start(
                out=wg1buf[:, 128 * kk:128 * kk + 128],
                in_=Wg1_in[128 * kk:128 * kk + 128, :]),
                writes=["wg1buf"], dma=True)
        P.do("sync", lambda e: e.dma_start(out=wg2buf[:, :], in_=Wg2_in[:, :]),
             writes=["wg2buf"], dma=True)

        for t1 in range(0) if SKIP_LATE else range(n_tiles_1):
            for j in range(TILE_1):
                cj = t1 * TILE_1 + j
                P.do("sync", lambda e, j=j, cj=cj: e.dma_start(
                    out=xnT_tile1[:, D * j:D * j + D], in_=xnT_f_d[128 * cj:128 * cj + 128, :]),
                    reads=["xnT_d"], writes=["xnT_tile"], dma=True)
                def g1_mms(e, j=j):
                    for kk in range(NKD):
                        inst = e.matmul(PSUM[0][:, :128],
                                        xnT_tile1[:, D * j + 128 * kk:D * j + 128 * kk + 128],
                                        wg1buf[:, 128 * kk:128 * kk + 128],
                                        start=(kk == 0), stop=(kk == NKD - 1))
                    return inst
                P.do("tensor", g1_mms, reads=["xnT_tile", "wg1buf"], writes=[PK[0]])
                P.do("vector", lambda e: e.tensor_copy(g1s[:, :], PSUM[0][:, :128]),
                     reads=[PK[0]], writes=["g1s"])
                P.do("tensor", lambda e, j=j: e.transpose(PSUM[1][:, :128], g1s[:, :],
                                                          ident[:, :]),
                     reads=["g1s", "ident"], writes=[PK[1]])
                P.do("vector", lambda e, j=j: e.tensor_copy(g1tile[:, 128 * j:128 * j + 128],
                                                            PSUM[1][:, :128]),
                     reads=[PK[1]], writes=["g1tile"])
            if t1 == 0 and (tp := tap("xnT_p1", [TILE_1 * 128, D])):
                for jj in range(TILE_1):
                    P.do("gpsimd", lambda e, jj=jj, tp=tp: e.dma_start(
                        out=tp[128 * jj:128 * jj + 128, :],
                        in_=xnT_tile1[:, D * jj:D * jj + D]),
                        reads=["xnT_tile"], writes=["tapst"], dma=True)
            for eb in range(NEB):
                wslice_load(wqbuf, "wqbuf", Wq_in, eb)
                for j in range(TILE_1):
                    cj = t1 * TILE_1 + j
                    par = (eb * TILE_1 + j) % 2
                    qstage = qstage2[par]; qTs = qTs2[par]; qts = qts2[par]
                    kTr = kTr2[par]; vr = vr2[par]; kvr = kvr2[par]
                    scsb = scsb2[par]; sgst = sgst2[par]; ogst = ogst2[par]
                    ssq4 = ssq42[par]; rinv4 = rinv42[par]; ogTst = ogTst2[par]
                    sfx = str(par)
                    P.do("sync", lambda e, cj=cj, eb=eb, kTr=kTr: e.dma_start(
                        out=kTr[:, :], in_=kT_d[128 * cj:128 * cj + 128, 512 * eb:512 * eb + 512]),
                        reads=["kT_d"], writes=["kTr" + sfx], dma=True)
                    P.do("sync", lambda e, cj=cj, eb=eb, vr=vr: e.dma_start(
                        out=vr[:, :], in_=v_d[128 * cj:128 * cj + 128, 512 * eb:512 * eb + 512]),
                        reads=["v_d"], writes=["vr" + sfx], dma=True)
                    P.do("sync", lambda e, cj=cj, eb=eb, kvr=kvr: e.dma_start(
                        out=kvr[:, :], in_=kv_d[128 * cj:128 * cj + 128, 512 * eb:512 * eb + 512]),
                        reads=["kv_d"], writes=["kvr" + sfx], dma=True)
                    def q_mms(e, j=j, par=par):
                        for kk in range(NKD):
                            inst = e.matmul(PSUM[2 + par][:, :],
                                            xnT_tile1[:, D * j + 128 * kk:D * j + 128 * kk + 128],
                                            wqbuf[:, 512 * kk:512 * kk + 512],
                                            start=(kk == 0), stop=(kk == NKD - 1))
                        return inst
                    P.do("tensor", q_mms, reads=["xnT_tile1", "wqbuf"], writes=[PK[2 + par]])
                    P.do("scalar", lambda e, qstage=qstage, par=par: e.activation(
                        qstage[:, :], PSUM[2 + par][:, :], AF.Silu),
                        reads=[PK[2 + par]], writes=["qstage" + sfx])
                    for hh in range(4):
                        P.do("tensor", lambda e, hh=hh, qstage=qstage, par=par: e.transpose(
                            PSUM[4 + par][:, 128 * hh:128 * hh + 128],
                            qstage[:, 128 * hh:128 * hh + 128], ident[:, :]),
                            reads=["qstage" + sfx, "ident"], writes=[PK[4 + par]])
                    P.do("vector", lambda e, qTs=qTs, par=par: e.tensor_copy(
                        qTs[:, :], PSUM[4 + par][:, :]),
                        reads=[PK[4 + par]], writes=["qTs" + sfx])
                    P.do("vector", lambda e, eb=eb, qts=qts, qTs=qTs: e.tensor_tensor(
                        qts[:, :], qTs[:, :], qdecTfull[:, 512 * eb:512 * eb + 512], ALU.mult),
                        reads=["qTs" + sfx, "qdecT"], writes=["qts" + sfx])
                    def sc_mms(e, kTr=kTr, qTs=qTs, par=par):
                        for hh in range(4):
                            inst = e.matmul(PSUM[4 + par][:, 128 * hh:128 * hh + 128],
                                            kTr[:, 128 * hh:128 * hh + 128],
                                            qTs[:, 128 * hh:128 * hh + 128],
                                            start=True, stop=True)
                        return inst
                    P.do("tensor", sc_mms, reads=["kTr" + sfx, "qTs" + sfx, "qTs" + sfx],
                         writes=[PK[4 + par]])
                    P.do("vector", lambda e, eb=eb, scsb=scsb, par=par: e.tensor_tensor(
                        scsb[:, :], PSUM[4 + par][:, :], DTT[:, 512 * eb:512 * eb + 512], ALU.mult),
                        reads=[PK[4 + par], "DTT"], writes=["scsb" + sfx])
                    def o_mms(e, eb=eb, scsb=scsb, vr=vr, qts=qts, par=par):
                        for hh in range(4):
                            e.matmul(PSUM[6 + par][:, 128 * hh:128 * hh + 128],
                                     scsb[:, 128 * hh:128 * hh + 128],
                                     vr[:, 128 * hh:128 * hh + 128],
                                     start=True, stop=False)
                            inst = e.matmul(PSUM[6 + par][:, 128 * hh:128 * hh + 128],
                                            qts[:, 128 * hh:128 * hh + 128],
                                            S_sb[:, 128 * (4 * eb + hh):128 * (4 * eb + hh) + 128],
                                            start=False, stop=True)
                        return inst
                    P.do("tensor", o_mms, reads=["scsb" + sfx, "vr" + sfx, "qts" + sfx, "S"],
                         writes=[PK[6 + par]])
                    P.do("vector", lambda e, eb=eb: e.tensor_tensor(
                        S_sb[:, 512 * eb:512 * eb + 512], S_sb[:, 512 * eb:512 * eb + 512],
                        cdfull[:, 512 * eb:512 * eb + 512], ALU.mult),
                        reads=["S", "cdfull"], writes=["S"])
                    P.do("vector", lambda e, eb=eb, kvr=kvr: e.tensor_tensor(
                        S_sb[:, 512 * eb:512 * eb + 512], S_sb[:, 512 * eb:512 * eb + 512],
                        kvr[:, :], ALU.add),
                        reads=["S", "kvr" + sfx], writes=["S"])
                    P.do("tensor", lambda e, j=j, eb=eb, par=par: e.matmul(
                        PSUM[par][:, :], g1tile[:, 128 * j:128 * j + 128],
                        wg2buf[:, 512 * eb:512 * eb + 512], start=True, stop=True),
                        reads=["g1tile", "wg2buf"], writes=[PK[par]])
                    P.do("scalar", lambda e, sgst=sgst, par=par: e.activation(
                        sgst[:, :], PSUM[par][:, :], AF.Sigmoid),
                        reads=[PK[par]], writes=["sgst" + sfx])
                    P.do("vector", lambda e, ogst=ogst, sgst=sgst, par=par: e.tensor_tensor(
                        ogst[:, :], PSUM[6 + par][:, :], sgst[:, :], ALU.mult),
                        reads=[PK[6 + par], "sgst" + sfx], writes=["ogst" + sfx])
                    for hh in range(4):
                        P.do("scalar", lambda e, hh=hh, ogst=ogst, ssq4=ssq4: e.activation(
                            sqh_b[:, 128 * hh:128 * hh + 128], ogst[:, 128 * hh:128 * hh + 128],
                            AF.Square, accum_out=ssq4[:, hh:hh + 1]),
                            reads=["ogst" + sfx], writes=["sqh", "ssq4" + sfx])
                    P.do("scalar", lambda e, ssq4=ssq4: e.activation(
                        ssq4[:, :], ssq4[:, :], AF.Sqrt, bias=epsv[:, :], scale=1.0 / HD),
                        reads=["ssq4" + sfx, "epsv"], writes=["ssq4" + sfx])
                    P.do("vector", lambda e, rinv4=rinv4, ssq4=ssq4: e.reciprocal(
                        rinv4[:, :], ssq4[:, :]),
                        reads=["ssq4" + sfx], writes=["rinv4" + sfx])
                    for hh in range(4):
                        P.do("vector", lambda e, hh=hh, ogst=ogst, rinv4=rinv4: e.tensor_single_scalar(
                            ogst[:, 128 * hh:128 * hh + 128], ogst[:, 128 * hh:128 * hh + 128],
                            rinv4[:, hh:hh + 1], ALU.mult),
                            reads=["ogst" + sfx, "rinv4" + sfx], writes=["ogst" + sfx])
                    for hh in range(4):
                        P.do("tensor", lambda e, hh=hh, ogst=ogst, par=par: e.transpose(
                            PSUM[6 + par][:, 128 * hh:128 * hh + 128],
                            ogst[:, 128 * hh:128 * hh + 128], ident[:, :]),
                            reads=["ogst" + sfx, "ident"], writes=[PK[6 + par]])
                    P.do("vector", lambda e, ogTst=ogTst, par=par: e.tensor_copy(
                        ogTst[:, :], PSUM[6 + par][:, :]),
                        reads=[PK[6 + par]], writes=["ogTst" + sfx])
                    P.do("gpsimd", lambda e, cj=cj, eb=eb, ogTst=ogTst: e.dma_start(
                        out=ogT_f_d[128 * cj:128 * cj + 128, 512 * eb:512 * eb + 512],
                        in_=ogTst[:, :]),
                        reads=["ogTst" + sfx], writes=["ogT_d"], dma=True)

        phase_close()
        if tp := tap("ogT_all", [n_chunks * 128, D]):
            P.do("sync", lambda e, tp=tp: e.dma_start(out=tp[:, :], in_=ogT_f_d[:, :]),
                 reads=["ogT_d"], writes=["tapst"], dma=True)
        if tp := tap("ogT0", [128, D]):
            P.do("sync", lambda e, tp=tp: e.dma_start(out=tp[:, :], in_=ogT_f_d[0:128, :]),
                 reads=["ogT_d"], writes=["tapst"], dma=True)
        if tp := tap("ogT1", [128, D]):
            P.do("sync", lambda e, tp=tp: e.dma_start(out=tp[:, :], in_=ogT_f_d[128:256, :]),
                 reads=["ogT_d"], writes=["tapst"], dma=True)

        P.scope("p2ai")
        phase_open()
        TILE_F1 = min(4, n_chunks)
        n_tiles_f1 = n_chunks // TILE_F1
        xq_x = sb("xq_x", [128, D], F32)
        xq_xn = sb("xq_xn", [128, D], F32)
        xq_u = sb("xq_u", [128, D], F32)
        xq_c = sb("xq_c", [128, D], I32)
        xq_dd = sb("xq_dd", [128, D], F32)
        xq_r = sb("xq_r", [128, D], F32)
        xq_bf = sb("xq_bf", [128, D], F32)
        xqT_tile = sb("xqT_tile", [128, TILE_F1 * D], BF16)
        q1r = sb("q1r", [128, 16], F32)
        ub1 = sb("ub1", [128, 1], F32)
        w1qTbuf = [sb("w1qTbuf%d" % i, [128, NKD * 512], BF16) for i in range(2)]
        fstt = sb("fstt", [128, 512], F32)
        hstage = sb("hstage", [128, 512], F32)
        hsq = sb("hsq", [128, 512], F32)
        b1slice = sb("b1slice", [128, 512], F32)
        hstat = sb("hstat", [128, 4 * TILE_F1], F32)
        st4 = sb("st4", [128, 4], F32)
        b1row = sb("b1row", [1, FF], F32)
        P.do("sync", lambda e: e.dma_start(out=b1row[:, :], in_=b1_in[:, :]),
             writes=["b1row"], dma=True)

        def quant_scalars(statcols, statkey, a_out_sb, a_col, a_key, sw_col):
            sc = statcols
            prefix = statkey
            P.do("vector", lambda e, sc=sc: e.tensor_single_scalar(
                q1r[:, 0:1], sc[0], 1.0 / sc[4], ALU.mult), reads=[prefix], writes=["q1r"])
            P.do("vector", lambda e, sc=sc: e.tensor_single_scalar(
                q1r[:, 1:2], sc[1], 1.0 / sc[4], ALU.mult), reads=[prefix], writes=["q1r"])
            P.do("vector", lambda e: e.tensor_tensor(q1r[:, 2:3], q1r[:, 0:1], q1r[:, 0:1],
                                                     ALU.mult), reads=["q1r"], writes=["q1r"])
            P.do("vector", lambda e: e.tensor_tensor(q1r[:, 1:2], q1r[:, 1:2], q1r[:, 2:3],
                                                     ALU.subtract), reads=["q1r"], writes=["q1r"])
            P.do("scalar", lambda e: e.activation(q1r[:, 2:3], q1r[:, 1:2], AF.Sqrt,
                                                  bias=epsv[:, :], scale=1.0),
                 reads=["q1r", "epsv"], writes=["q1r"])
            P.do("vector", lambda e: e.reciprocal(q1r[:, 3:4], q1r[:, 2:3]),
                 reads=["q1r"], writes=["q1r"])
            P.do("vector", lambda e, sc=sc: e.tensor_tensor(q1r[:, 4:5], sc[2], q1r[:, 0:1],
                                                            ALU.subtract),
                 reads=[prefix, "q1r"], writes=["q1r"])
            P.do("vector", lambda e, sc=sc: e.tensor_tensor(q1r[:, 5:6], q1r[:, 0:1], sc[3],
                                                            ALU.subtract),
                 reads=["q1r", prefix], writes=["q1r"])
            P.do("vector", lambda e: e.tensor_tensor(q1r[:, 4:5], q1r[:, 4:5], q1r[:, 5:6],
                                                     ALU.max), reads=["q1r"], writes=["q1r"])
            P.do("vector", lambda e: e.tensor_tensor(q1r[:, 4:5], q1r[:, 4:5], q1r[:, 3:4],
                                                     ALU.mult), reads=["q1r"], writes=["q1r"])
            P.do("vector", lambda e: e.tensor_single_scalar(q1r[:, 4:5], q1r[:, 4:5], EPS,
                                                            ALU.max), reads=["q1r"], writes=["q1r"])
            P.do("vector", lambda e: e.reciprocal(q1r[:, 5:6], q1r[:, 4:5]),
                 reads=["q1r"], writes=["q1r"])
            P.do("vector", lambda e: e.tensor_scalar(q1r[:, 6:7], q1r[:, 5:6], 127.0, None,
                                                     ALU.mult, ALU.bypass),
                 reads=["q1r"], writes=["q1r"])
            P.do("vector", lambda e: e.tensor_tensor(q1r[:, 6:7], q1r[:, 6:7], q1r[:, 3:4],
                                                     ALU.mult), reads=["q1r"], writes=["q1r"])
            P.do("vector", lambda e: e.tensor_tensor(ub1[:, :], q1r[:, 6:7], q1r[:, 0:1],
                                                     ALU.mult), reads=["q1r"], writes=["ub1"])
            P.do("vector", lambda e: e.tensor_scalar(ub1[:, :], ub1[:, :], -1.0, 16384.0,
                                                     ALU.mult, ALU.add),
                 reads=["ub1"], writes=["ub1"])
            P.do("vector", lambda e, sw_col=sw_col: e.tensor_tensor(
                q1r[:, 7:8], q1r[:, 4:5], swb[:, sw_col:sw_col + 1], ALU.mult),
                reads=["q1r", "swb"], writes=["q1r"])
            P.do("vector", lambda e, a_out_sb=a_out_sb, a_col=a_col: e.tensor_scalar(
                a_out_sb[:, a_col:a_col + 1], q1r[:, 7:8], -1.0 / 127.0, None,
                ALU.mult, ALU.bypass),
                reads=["q1r"], writes=[a_key])

        def emit_round(src, srckey, n):
            P.do("vector", lambda e, n=n: e.tensor_copy(xq_c[:, :n], src[:, :n]),
                 reads=[srckey], writes=["xq_c"])
            P.do("vector", lambda e, n=n: e.tensor_tensor(xq_dd[:, :n], src[:, :n],
                                                          xq_c[:, :n], ALU.subtract),
                 reads=[srckey, "xq_c"], writes=["xq_dd"])
            P.do("vector", lambda e, n=n: e.scalar_tensor_tensor(
                xq_r[:, :n], xq_dd[:, :n], 0.5, xq_c[:, :n], ALU.is_gt, ALU.add),
                reads=["xq_dd", "xq_c"], writes=["xq_r"])
            P.do("vector", lambda e, n=n: e.tensor_single_scalar(
                xq_bf[:, :n], xq_r[:, :n], -16384.0, ALU.add),
                reads=["xq_r"], writes=["xq_bf"])

        for tf in range([] if SKIP_LATE else range(n_tiles_f1)) if False else (range(0) if SKIP_LATE else range(n_tiles_f1)):
            for j in range(TILE_F1):
                cj = tf * TILE_F1 + j
                P.do("sync", lambda e, cj=cj: e.dma_start(out=xq_x[:, :],
                                                          in_=x_in[C * cj:C * cj + C, :]),
                     writes=["xq_x"], dma=True)
                P.do("vector", lambda e, cj=cj: e.tensor_scalar(
                    xq_xn[:, :], xq_x[:, :], mu_sb[:, cj:cj + 1], rstd_sb[:, cj:cj + 1],
                    ALU.subtract, ALU.mult),
                    reads=["xq_x", "mu", "rstd"], writes=["xq_xn"])
                P.do("vector", lambda e: e.reduce_sum(st4[:, 0:1], xq_xn[:, :], axis=AX),
                     reads=["xq_xn"], writes=["st4"])
                P.do("scalar", lambda e: e.activation(xq_u[:, :], xq_xn[:, :], AF.Square,
                                                      accum_out=st4[:, 1:2]),
                     reads=["xq_xn"], writes=["st4", "xq_u"])
                P.do("vector", lambda e: e.reduce_max(st4[:, 2:3], xq_xn[:, :], axis=AX),
                     reads=["xq_xn"], writes=["st4"])
                P.do("vector", lambda e: e.tensor_reduce(st4[:, 3:4], xq_xn[:, :], AX, ALU.min),
                     reads=["xq_xn"], writes=["st4"])
                quant_scalars((st4[:, 0:1], st4[:, 1:2], st4[:, 2:3], st4[:, 3:4], D),
                              "st4", a1_sb, cj, "a1m", 1)
                P.do("scalar", lambda e: e.activation(xq_u[:, :], xq_xn[:, :], AF.Relu,
                                                      bias=ub1[:, :], scale=q1r[:, 6:7]),
                     reads=["xq_xn", "ub1", "q1r"], writes=["xq_u"])
                emit_round(xq_u, "xq_u", D)
                for b in range(NKD):
                    P.do("tensor", lambda e, b=b: e.transpose(
                        PSUM[b // 4][:, 128 * (b % 4):128 * (b % 4) + 128],
                        xq_bf[:, 128 * b:128 * b + 128], ident[:, :]),
                        reads=["xq_bf", "ident"], writes=[PK[b // 4]])
                for q4 in range(4):
                    P.do("vector", lambda e, q4=q4, j=j: e.tensor_copy(
                        xqT_tile[:, D * j + 512 * q4:D * j + 512 * q4 + 512], PSUM[q4][:, :]),
                        reads=[PK[q4]], writes=["xqT_tile"])
            for fb in range(NFB):
                w1b = w1qTbuf[fb % 2]
                w1k = "w1qTbuf%d" % (fb % 2)
                for kk in range(NKD):
                    P.do("sync", lambda e, fb=fb, kk=kk, w1b=w1b: e.dma_start_transpose(
                        w1b[:, 512 * kk:512 * kk + 512],
                        w1q_d[512 * fb:512 * fb + 512, 128 * kk:128 * kk + 128]),
                        reads=["wq_d0"], writes=[w1k], dma=True)
                P.do("tensor", lambda e, fb=fb: e.matmul(
                    PSUM[4][:, :], ones_row[:1, :128], b1row[:1, 512 * fb:512 * fb + 512],
                    start=True, stop=True),
                    reads=["ones_row", "b1row"], writes=[PK[4]])
                P.do("vector", lambda e: e.tensor_copy(b1slice[:, :], PSUM[4][:, :]),
                     reads=[PK[4]], writes=["b1slice"])
                for j in range(TILE_F1):
                    cj = tf * TILE_F1 + j

                    def f1_mms(e, j=j, w1b=w1b):
                        for kk in range(NKD):
                            inst = e.matmul(PSUM[5][:, :],
                                            xqT_tile[:, D * j + 128 * kk:D * j + 128 * kk + 128],
                                            w1b[:, 512 * kk:512 * kk + 512],
                                            start=(kk == 0), stop=(kk == NKD - 1))
                        return inst
                    P.do("tensor", f1_mms, reads=["xqT_tile", w1k], writes=[PK[5]])
                    P.do("vector", lambda e, cj=cj: e.scalar_tensor_tensor(
                        fstt[:, :], PSUM[5][:, :], a1_sb[:, cj:cj + 1], b1slice[:, :],
                        ALU.mult, ALU.add),
                        reads=[PK[5], "a1m", "b1slice"], writes=["fstt"])
                    P.do("scalar", lambda e, j=j: e.activation(hstage[:, :], fstt[:, :], AF.Gelu,
                                                               accum_out=st4[:, 0:1]),
                         reads=["fstt"], writes=["hstage", "st4"])
                    P.do("scalar", lambda e: e.activation(hsq[:, :], hstage[:, :], AF.Square,
                                                          accum_out=st4[:, 1:2]),
                         reads=["hstage"], writes=["hsq", "st4"])
                    P.do("vector", lambda e: e.reduce_max(st4[:, 2:3], hstage[:, :], axis=AX),
                         reads=["hstage"], writes=["st4"])
                    P.do("vector", lambda e: e.tensor_reduce(st4[:, 3:4], hstage[:, :], AX,
                                                             ALU.min),
                         reads=["hstage"], writes=["st4"])
                    if fb == 0:
                        P.do("vector", lambda e, j=j: e.tensor_copy(
                            hstat[:, 4 * j:4 * j + 4], st4[:, :]),
                            reads=["st4"], writes=["hstat"])
                    else:
                        for si, alu in ((0, ALU.add), (1, ALU.add), (2, ALU.max), (3, ALU.min)):
                            P.do("vector", lambda e, j=j, si=si, alu=alu: e.tensor_tensor(
                                hstat[:, 4 * j + si:4 * j + si + 1],
                                hstat[:, 4 * j + si:4 * j + si + 1], st4[:, si:si + 1], alu),
                                reads=["hstat", "st4"], writes=["hstat"])
                    P.do("gpsimd", lambda e, cj=cj, fb=fb: e.dma_start(
                        out=h_d[128 * cj:128 * cj + 128, 512 * fb:512 * fb + 512],
                        in_=hstage[:, :]),
                        reads=["hstage"], writes=["h_d"], dma=True)
            for j in range(TILE_F1):
                cj = tf * TILE_F1 + j
                quant_scalars((hstat[:, 4 * j:4 * j + 1], hstat[:, 4 * j + 1:4 * j + 2],
                               hstat[:, 4 * j + 2:4 * j + 3], hstat[:, 4 * j + 3:4 * j + 4],
                               FF), "hstat", a2_sb, cj, "a2m", 3)
                P.do("vector", lambda e, cj=cj: e.tensor_copy(aq2_sb[:, cj:cj + 1],
                                                              q1r[:, 6:7]),
                     reads=["q1r"], writes=["aq2"])
                P.do("vector", lambda e, cj=cj: e.tensor_copy(ub2_sb[:, cj:cj + 1], ub1[:, :]),
                     reads=["ub1"], writes=["ub2"])
        if tp := tap("h0", [128, FF]):
            P.do("sync", lambda e, tp=tp: e.dma_start(out=tp[:, :], in_=h_d[0:128, :]),
                 reads=["h_d"], writes=["tapst"], dma=True)
        phase_close()

        P.scope("p2aii")
        phase_open()
        TILE_F2 = min(2, n_chunks)
        n_tiles_f2 = n_chunks // TILE_F2
        h_r = sb("h_r", [128, D], F32)
        h_u = sb("h_u", [128, D], F32)
        h_c = sb("h_c", [128, D], I32)
        h_dd = sb("h_dd", [128, D], F32)
        h_rr = sb("h_rr", [128, D], F32)
        h_bf = sb("h_bf", [128, D], F32)
        hqT_tile = sb("hqT_tile", [128, TILE_F2 * NKF * 128], BF16)
        w2qTbuf = [sb("w2qTbuf%d" % i, [128, D], BF16) for i in range(2)]
        b2s4 = sb("b2s4", [128, 4 * 512], F32)
        g2stt = sb("g2stt", [128, 512], F32)
        gstage = sb("gstage", [128, 512], F32)
        b2row = sb("b2row", [1, D], F32)
        P.do("sync", lambda e: e.dma_start(out=b2row[:, :], in_=b2_in[:, :]),
             writes=["b2row"], dma=True)

        for db in range(4):
            P.do("tensor", lambda e, db=db: e.matmul(
                PSUM[0][:, :], ones_row[:1, :128], b2row[:1, 512 * db:512 * db + 512],
                start=True, stop=True),
                reads=["ones_row", "b2row"], writes=[PK[0]])
            P.do("vector", lambda e, db=db: e.tensor_copy(b2s4[:, 512 * db:512 * db + 512],
                                                          PSUM[0][:, :]),
                 reads=[PK[0]], writes=["b2s4"])

        for tf in range(0) if SKIP_LATE else range(n_tiles_f2):
            for j in range(TILE_F2):
                cj = tf * TILE_F2 + j
                for ss in range(FF // D):
                    P.do("sync", lambda e, cj=cj, ss=ss: e.dma_start(
                        out=h_r[:, :], in_=h_d[128 * cj:128 * cj + 128, D * ss:D * ss + D]),
                        reads=["h_d"], writes=["h_r"], dma=True)
                    P.do("scalar", lambda e, cj=cj: e.activation(
                        h_u[:, :], h_r[:, :], AF.Relu, bias=ub2_sb[:, cj:cj + 1],
                        scale=aq2_sb[:, cj:cj + 1]),
                        reads=["h_r", "ub2", "aq2"], writes=["h_u"])
                    P.do("vector", lambda e: e.tensor_copy(h_c[:, :], h_u[:, :]),
                         reads=["h_u"], writes=["h_c"])
                    P.do("vector", lambda e: e.tensor_tensor(h_dd[:, :], h_u[:, :], h_c[:, :],
                                                             ALU.subtract),
                         reads=["h_u", "h_c"], writes=["h_dd"])
                    P.do("vector", lambda e: e.scalar_tensor_tensor(
                        h_rr[:, :], h_dd[:, :], 0.5, h_c[:, :], ALU.is_gt, ALU.add),
                        reads=["h_dd", "h_c"], writes=["h_rr"])
                    P.do("vector", lambda e: e.tensor_single_scalar(
                        h_bf[:, :], h_rr[:, :], -16384.0, ALU.add),
                        reads=["h_rr"], writes=["h_bf"])
                    for b in range(NKD):
                        P.do("tensor", lambda e, b=b: e.transpose(
                            PSUM[b // 4][:, 128 * (b % 4):128 * (b % 4) + 128],
                            h_bf[:, 128 * b:128 * b + 128], ident[:, :]),
                            reads=["h_bf", "ident"], writes=[PK[b // 4]])
                    for q4 in range(4):
                        P.do("vector", lambda e, q4=q4, j=j, ss=ss: e.tensor_copy(
                            hqT_tile[:, (j * NKF + ss * 16 + q4 * 4) * 128:
                                     (j * NKF + ss * 16 + q4 * 4 + 4) * 128],
                            PSUM[q4][:, :]),
                            reads=[PK[q4]], writes=["hqT_tile"])
            for ff in range(NKF):
                buf = w2qTbuf[ff % 2]
                bk = "w2qT%d" % (ff % 2)
                P.do("scalar", lambda e, ff=ff, buf=buf: e.dma_start_transpose(
                    buf[:, :], w2q_d[:, 128 * ff:128 * ff + 128]),
                    reads=["wq_d1"], writes=[bk], dma=True)

                def f2_mms(e, ff=ff, buf=buf):
                    for j in range(TILE_F2):
                        for db in range(4):
                            inst = e.matmul(
                                PSUM[4 * j + db][:, :],
                                hqT_tile[:, (j * NKF + ff) * 128:(j * NKF + ff) * 128 + 128],
                                buf[:, 512 * db:512 * db + 512],
                                start=(ff == 0), stop=(ff == NKF - 1),
                                skip_group_check=True)
                    return inst
                P.do("tensor", f2_mms, reads=["hqT_tile", bk],
                     writes=[PK[i] for i in range(8)])
            for j in range(TILE_F2):
                cj = tf * TILE_F2 + j
                for db in range(4):
                    P.do("vector", lambda e, cj=cj, j=j, db=db: e.scalar_tensor_tensor(
                        g2stt[:, :], PSUM[4 * j + db][:, :], a2_sb[:, cj:cj + 1],
                        b2s4[:, 512 * db:512 * db + 512], ALU.mult, ALU.add),
                        reads=[PK[4 * j + db], "a2m", "b2s4"], writes=["g2stt"])
                    P.do("scalar", lambda e: e.activation(gstage[:, :], g2stt[:, :], AF.Silu),
                         reads=["g2stt"], writes=["gstage"])
                    P.do("gpsimd", lambda e, cj=cj, db=db: e.dma_start(
                        out=g_d[128 * cj:128 * cj + 128, 512 * db:512 * db + 512],
                        in_=gstage[:, :]),
                        reads=["gstage"], writes=["g_d"], dma=True)
        if tp := tap("g0", [128, D]):
            P.do("sync", lambda e, tp=tp: e.dma_start(out=tp[:, :], in_=g_d[0:128, :]),
                 reads=["g_d"], writes=["tapst"], dma=True)
        if tp := tap("g_all", [n_chunks * 128, D]):
            P.do("sync", lambda e, tp=tp: e.dma_start(out=tp[:, :], in_=g_d[:, :]),
                 reads=["g_d"], writes=["tapst"], dma=True)
        if tp := tap("h_all", [n_chunks * 128, FF]):
            P.do("sync", lambda e, tp=tp: e.dma_start(out=tp[:, :], in_=h_d[:, :]),
                 reads=["h_d"], writes=["tapst"], dma=True)
        phase_close()

        P.scope("p2b")
        phase_open()
        wobuf = sb("wobuf", [128, NKD * D], F32)
        nwcol = sb("nwcol", [128, NKD], F32)
        nwt16 = sb("nwt16", [16, 128], F32)
        ogTr = sb("ogTr", [128, D], F32)
        gr = sb("gr", [128, D], F32)
        xr2 = sb("xr2", [128, D], F32)
        ystage = sb("ystage", [128, D], F32)

        P.do("sync", lambda e: e.dma_start(out=nwt16[:, :],
                                           in_=nw_in[:, :].rearrange("o (b c) -> (o b) c", b=16)),
             writes=["nwt16"], dma=True)
        P.do("tensor", lambda e: e.transpose(PSUM[0][:, :16], nwt16[:, :], ident[:16, :16]),
             reads=["nwt16", "ident"], writes=[PK[0]])
        P.do("vector", lambda e: e.tensor_copy(nwcol[:, :], PSUM[0][:, :16]),
             reads=[PK[0]], writes=["nwcol"])
        for kk in range(NKD):
            P.do("sync", lambda e, kk=kk: e.dma_start(
                out=wobuf[:, D * kk:D * kk + D],
                in_=Wo_in[128 * kk:128 * kk + 128, :]),
                writes=["wobuf"], dma=True)
        for kk in range(NKD):
            P.do("vector", lambda e, kk=kk: e.tensor_single_scalar(
                wobuf[:, D * kk:D * kk + D], wobuf[:, D * kk:D * kk + D],
                nwcol[:, kk:kk + 1], ALU.mult),
                reads=["wobuf", "nwcol"], writes=["wobuf"])

        for cj in range(0) if SKIP_LATE else range(n_chunks):
            P.do("sync", lambda e, cj=cj: e.dma_start(out=ogTr[:, :],
                                                      in_=ogT_f_d[128 * cj:128 * cj + 128, :]),
                 reads=["ogT_d"], writes=["ogTr"], dma=True)
            P.do("sync", lambda e, cj=cj: e.dma_start(out=gr[:, :],
                                                      in_=g_d[128 * cj:128 * cj + 128, :]),
                 reads=["g_d"], writes=["gr"], dma=True)
            P.do("sync", lambda e, cj=cj: e.dma_start(out=xr2[:, :],
                                                      in_=x_in[C * cj:C * cj + C, :]),
                 writes=["xr2"], dma=True)
            for eb in range(NEB):
                def wo_mms(e, eb=eb):
                    for kk in range(NKD):
                        inst = e.matmul(PSUM[4 + (eb % 2)][:, :],
                                        ogTr[:, 128 * kk:128 * kk + 128],
                                        wobuf[:, D * kk + 512 * eb:D * kk + 512 * eb + 512],
                                        start=(kk == 0), stop=(kk == NKD - 1))
                    return inst
                P.do("tensor", wo_mms, reads=["ogTr", "wobuf"], writes=[PK[4 + (eb % 2)]])
                P.do("vector", lambda e, eb=eb: e.tensor_tensor(
                    ystage[:, 512 * eb:512 * eb + 512], PSUM[4 + (eb % 2)][:, :],
                    gr[:, 512 * eb:512 * eb + 512], ALU.mult),
                    reads=[PK[4 + (eb % 2)], "gr"], writes=["ystage"])
            P.do("vector", lambda e: e.tensor_tensor(ystage[:, :], ystage[:, :], xr2[:, :],
                                                     ALU.add),
                 reads=["ystage", "xr2"], writes=["ystage"])
            P.do("gpsimd", lambda e, cj=cj: e.dma_start(out=y_out[C * cj:C * cj + C, :],
                                                        in_=ystage[:, :]),
                 reads=["ystage"], writes=["y_d"], dma=True)
        phase_close()

        with nc.Block() as block:
            P.finalize_and_replay(block)

    return nc, tap_outs


N_CORES = 8
SPANS = 4
N_CH = 32
T_SPAN = N_CH * C

_prog_cache = {}


def _get_prog():
    if "nc" not in _prog_cache:
        nc, _ = build(n_chunks=N_CH, n_cores=N_CORES, taps=(), mm="f32")
        _prog_cache["nc"] = nc
    return _prog_cache["nc"]


def kernel(L0_attn_x, L0_attn_log_decay, L0_attn_Wq, L0_attn_Wk, L0_attn_Wv,
           L0_attn_Wo, L0_attn_Wg1, L0_attn_Wg2, L0_attn_norm_w,
           L0_ffn_w1, L0_ffn_b1, L0_ffn_w2, L0_ffn_b2):
    x = np.ascontiguousarray(np.asarray(L0_attn_x, np.float32))
    B, N, _ = x.shape
    ld = np.asarray(L0_attn_log_decay, np.float32).reshape(1, H)
    base = {
        "log_decay": ld,
        "Wq": np.ascontiguousarray(np.asarray(L0_attn_Wq, np.float32)),
        "Wk": np.ascontiguousarray(np.asarray(L0_attn_Wk, np.float32)),
        "Wv": np.ascontiguousarray(np.asarray(L0_attn_Wv, np.float32)),
        "Wo": np.ascontiguousarray(np.asarray(L0_attn_Wo, np.float32)),
        "Wg1": np.ascontiguousarray(np.asarray(L0_attn_Wg1, np.float32)),
        "Wg2": np.ascontiguousarray(np.asarray(L0_attn_Wg2, np.float32)),
        "norm_w": np.asarray(L0_attn_norm_w, np.float32).reshape(1, D),
        "w1": np.ascontiguousarray(np.asarray(L0_ffn_w1, np.float32)),
        "b1": np.asarray(L0_ffn_b1, np.float32).reshape(1, FF),
        "w2": np.ascontiguousarray(np.asarray(L0_ffn_w2, np.float32)),
        "b2": np.asarray(L0_ffn_b2, np.float32).reshape(1, D),
    }
    nc = _get_prog()
    in_maps = []
    for core in range(N_CORES):
        b, p = divmod(core, SPANS)
        sdel = np.full((1, N_CORES), HUGE_DELTA, np.float32)
        for kk in range(p):
            sdel[0, b * SPANS + kk] = T_SPAN * (p - 1 - kk)
        im = dict(base)
        im["x"] = np.ascontiguousarray(x[b, p * T_SPAN:(p + 1) * T_SPAN])
        im["span_deltas"] = sdel
        in_maps.append(im)
    res = run_bass_kernel_spmd(nc, in_maps, core_ids=list(range(N_CORES)))
    y = np.stack([np.concatenate([res.results[b * SPANS + p]["y"] for p in range(SPANS)], 0)
                  for b in range(B)], 0)
    return y


# version 05
# speedup vs baseline: 1.0099x; 1.0099x over previous
import numpy as np
from contextlib import ExitStack
import concourse.bass as bass
import concourse.mybir as mybir
from concourse.bass_utils import run_bass_kernel_spmd

F32 = mybir.dt.float32
BF16 = mybir.dt.bfloat16
I32 = mybir.dt.int32
AF = mybir.ActivationFunctionType
ALU = mybir.AluOpType
AX = mybir.AxisListType.X

D = 2048
H = 16
HD = 128
C = 128
FF = 8192
NKD = D // 128
NEB = D // 512
NFB = FF // 512
NKF = FF // 128
EPS = 1e-5
HUGE_DELTA = 1e7


class Op:
    __slots__ = ("eng", "fn", "reads", "writes", "dma", "waits", "need_inc",
                 "need_drain", "semval", "idx", "scope")

    def __init__(self, eng, fn, reads, writes, dma, scope=""):
        self.eng, self.fn = eng, fn
        self.reads, self.writes, self.dma = reads, writes, dma
        self.waits = []
        self.need_inc = False
        self.need_drain = False
        self.semval = None
        self.idx = None
        self.scope = scope


class Barrier:
    pass


class Prog:
    ENGINES = ("sync", "scalar", "vector", "tensor", "gpsimd")

    def __init__(self, nc, stack):
        self.nc = nc
        self.stack = stack
        self.ops = []
        self.last_w = {}
        self.readers = {}
        self.n_bar = 0

    def scope(self, name):
        self.cur_scope = name

    def do(self, eng, fn, reads=(), writes=(), dma=False):
        op = Op(eng, fn, tuple(reads), tuple(writes), dma, getattr(self, "cur_scope", ""))
        deps = []
        for r in op.reads:
            p = self.last_w.get(r)
            if p is not None:
                deps.append(p)
        for w in op.writes:
            p = self.last_w.get(w)
            if p is not None:
                deps.append(p)
            deps.extend(self.readers.get(w, ()))
        for p in deps:
            if p is op:
                continue
            if p.eng == eng:
                if eng != "tensor" and not (op.dma and p.dma):
                    op.need_drain = True
            else:
                p.need_inc = True
                op.waits.append(p)
        for r in op.reads:
            self.readers.setdefault(r, []).append(op)
        for w in op.writes:
            self.last_w[w] = op
            self.readers[w] = []
        self.ops.append(op)
        return op

    def barrier(self):
        self.ops.append(Barrier())
        self.last_w = {}
        self.readers = {}

    def finalize_and_replay(self, block):
        sems = {}
        for e in self.ENGINES:
            sems[e] = self.stack.enter_context(self.nc.semaphore("sem_" + e))
        barsem = self.stack.enter_context(self.nc.semaphore("sem_bar"))

        n_ops = len(self.ops)
        last_before_bar = {}
        for i, op in enumerate(self.ops):
            if isinstance(op, Barrier):
                for e, p in last_before_bar.items():
                    p.need_inc = True
                last_before_bar = {}
            else:
                last_before_bar[op.eng] = op
        for op in self.ops:
            if not isinstance(op, Barrier) and op.dma:
                op.need_inc = True
        counts = {e: 0 for e in self.ENGINES}
        bar_marks = []
        for op in self.ops:
            if isinstance(op, Barrier):
                bar_marks.append(dict(counts))
                continue
            if op.need_inc:
                counts[op.eng] += 16 if op.dma else 1
                op.semval = counts[op.eng]

        per_engine = {e: [] for e in self.ENGINES}
        bar_i = [0]
        bar_total = [0]
        pending_waits = {e: {} for e in self.ENGINES}
        bar_floor = {e: {} for e in self.ENGINES}

        bar_count = 0
        floors = {e: {s: 0 for s in self.ENGINES} for e in self.ENGINES}
        for op in self.ops:
            if isinstance(op, Barrier):
                bar_count += 1
                snap = bar_marks[bar_count - 1]
                target = bar_count * len(self.ENGINES)
                for e in self.ENGINES:
                    def emit_bar(eng_obj, snap=snap, target=target, e=e):
                        for s in self.ENGINES:
                            if s != e and snap[s] > 0:
                                eng_obj.wait_ge(sems[s], snap[s])
                        eng_obj.nop().then_inc(barsem, 1)
                        eng_obj.wait_ge(barsem, target)
                    per_engine[e].append(("bar", emit_bar))
                    for s in self.ENGINES:
                        floors[e][s] = max(floors[e][s], snap[s])
                continue
            waits = {}
            for p in op.waits:
                if p.semval is None:
                    continue
                if p.semval <= floors[op.eng][p.eng]:
                    continue
                waits[p.eng] = max(waits.get(p.eng, 0), p.semval)

            def emit(eng_obj, op=op, waits=waits):
                for s, v in waits.items():
                    eng_obj.wait_ge(sems[s], v)
                if op.need_drain:
                    eng_obj.drain()
                instr = op.fn(eng_obj)
                if op.need_inc:
                    assert instr is not None, "op needs inc but returned no instruction"
                    instr.then_inc(sems[op.eng], 16 if op.dma else 1)
            per_engine[op.eng].append((op.scope, emit))

        nc = self.nc
        for engname in self.ENGINES:
            ops = per_engine[engname]
            def section(e, ops=ops):
                cur = [None, None]
                for sc, f in ops:
                    if sc != cur[0]:
                        if cur[0]:
                            nc.leave_named_scope(cur[0], cur[1], False)
                        if sc:
                            cur[1], _ = nc.enter_named_scope(sc, False)
                        cur[0] = sc
                    f(e)
                if cur[0]:
                    nc.leave_named_scope(cur[0], cur[1], False)
            getattr(block, engname)(section)


DIAG_BARRIER = False
SKIP_PREP = False
SKIP_LATE = False
SKIP_A2_BODY = False
A2_ONLY_WLOADS = False


def build(n_chunks=32, n_cores=8, taps=(), mm="f32"):
    nc = bass.Bass("TRN2", target_bir_lowering=False)
    T = n_chunks * C

    x_in = nc.dram_tensor("x", [T, D], F32, kind="ExternalInput").ap()
    ld_in = nc.dram_tensor("log_decay", [1, H], F32, kind="ExternalInput").ap()
    Wq_in = nc.dram_tensor("Wq", [D, D], F32, kind="ExternalInput").ap()
    Wk_in = nc.dram_tensor("Wk", [D, D], F32, kind="ExternalInput").ap()
    Wv_in = nc.dram_tensor("Wv", [D, D], F32, kind="ExternalInput").ap()
    Wo_in = nc.dram_tensor("Wo", [D, D], F32, kind="ExternalInput").ap()
    Wg1_in = nc.dram_tensor("Wg1", [D, HD], F32, kind="ExternalInput").ap()
    Wg2_in = nc.dram_tensor("Wg2", [HD, D], F32, kind="ExternalInput").ap()
    nw_in = nc.dram_tensor("norm_w", [1, D], F32, kind="ExternalInput").ap()
    w1_in = nc.dram_tensor("w1", [FF, D], F32, kind="ExternalInput").ap()
    b1_in = nc.dram_tensor("b1", [1, FF], F32, kind="ExternalInput").ap()
    w2_in = nc.dram_tensor("w2", [D, FF], F32, kind="ExternalInput").ap()
    b2_in = nc.dram_tensor("b2", [1, D], F32, kind="ExternalInput").ap()
    sdel_in = nc.dram_tensor("span_deltas", [1, n_cores], F32, kind="ExternalInput").ap()
    y_out = nc.dram_tensor("y", [T, D], F32, kind="ExternalOutput").ap()

    tap_outs = {}

    def tap(name, shape, dtype=F32):
        if name in taps:
            tap_outs[name] = nc.dram_tensor("tap_" + name, shape, dtype,
                                            kind="ExternalOutput").ap()
            return tap_outs[name]
        return None

    xnT_f_d = nc.dram_tensor("xnT_f_d", [n_chunks * 128, D], F32).ap()
    kT_d = nc.dram_tensor("kT_d", [n_chunks * 128, D], F32).ap()
    v_d = nc.dram_tensor("v_d", [n_chunks * 128, D], F32).ap()
    kv_d = nc.dram_tensor("kv_d", [n_chunks * 128, H * HD], F32).ap()
    ogT_f_d = nc.dram_tensor("ogT_f_d", [n_chunks * 128, D], F32).ap()
    w1q_d = nc.dram_tensor("w1q_d", [FF, D], BF16).ap()
    w2q_d = nc.dram_tensor("w2q_d", [D, FF], BF16).ap()
    h_d = nc.dram_tensor("h_d", [T, FF], F32).ap()
    g_d = nc.dram_tensor("g_d", [T, D], F32).ap()
    cc_in_d = nc.dram_tensor("cc_in_d", [128, H * HD], F32).ap()
    cc_out_d = nc.dram_tensor("cc_out_d", [n_cores * 128, H * HD], F32).ap()

    stack = ExitStack()
    P = Prog(nc, stack)
    cur_stack = [stack]

    def sb(name, shape, dtype=F32):
        return cur_stack[-1].enter_context(nc.sbuf_tensor(name, shape, dtype))

    def phase_open():
        cur_stack.append(ExitStack())

    def phase_close():
        P.barrier()
        cur_stack.pop().close()

    with stack:
        ident = sb("ident", [128, 128], F32)
        ldb = sb("ldb", [128, H], F32)
        DTT = sb("DTT", [128, H * 128], F32)
        qdecTfull = sb("qdecTfull", [128, D], F32)
        cdbc = sb("cdbc", [128, H], F32)
        cdfull = sb("cdfull", [128, D], F32)
        coefsh = sb("coefsh", [128, n_cores * H], F32)
        kdec1 = sb("kdec1", [128, H], F32)
        epsv = sb("epsv", [128, 1], F32)
        sdel_sb = sb("sdel_sb", [128, n_cores], F32)
        ones_row = sb("ones_row", [1, 128], F32)
        onesc = sb("onesc", [128, 128], F32)
        mu_sb = sb("mu_sb", [128, max(n_chunks, 1)], F32)
        rstd_sb = sb("rstd_sb", [128, max(n_chunks, 1)], F32)
        S_sb = sb("S_sb", [128, H * HD], F32)
        swb = sb("swb", [128, 4], F32)
        a1_sb = sb("a1_sb", [128, max(n_chunks, 1)], F32)
        a2_sb = sb("a2_sb", [128, max(n_chunks, 1)], F32)
        aq2_sb = sb("aq2_sb", [128, max(n_chunks, 1)], F32)
        ub2_sb = sb("ub2_sb", [128, max(n_chunks, 1)], F32)
        identb = sb("identb", [128, 128], BF16)

        PSUM = [stack.enter_context(nc.psum_tensor("pb%d" % i, [128, 512], F32))
                for i in range(8)]
        PK = ["psum%d" % i for i in range(8)]

        P.scope("consts")
        phase_open()
        P.do("sync", lambda e: e.dma_start(out=ldb[:1, :], in_=ld_in[:, :]),
             writes=["ldb_row"], dma=True)
        P.do("sync", lambda e: e.dma_start(out=sdel_sb[:1, :], in_=sdel_in[:, :]),
             writes=["sdel_row"], dma=True)

        it_a = sb("it_a", [128, 128], F32)
        relu_d = sb("relu_d", [128, 128], F32)
        P.do("gpsimd", lambda e: e.iota(it_a[:, :], pattern=[[1, 128]], channel_multiplier=-1,
                                        allow_small_or_imprecise_dtypes=True), writes=["it_a"])
        P.do("vector", lambda e: e.tensor_single_scalar(ident[:, :], it_a[:, :], 0.0, ALU.is_equal),
             reads=["it_a"], writes=["ident"])
        P.do("vector", lambda e: e.tensor_copy(identb[:, :], ident[:, :]),
             reads=["ident"], writes=["identb"])
        P.do("vector", lambda e: e.memset(ones_row[:, :], 1.0), writes=["ones_row"])
        P.do("vector", lambda e: e.memset(onesc[:, :], 1.0), writes=["onesc"])
        P.do("vector", lambda e: e.memset(epsv[:, :], EPS), writes=["epsv"])

        P.do("tensor", lambda e: e.matmul(PSUM[0][:, :H], ones_row[:1, :128], ldb[:1, :H],
                                          start=True, stop=True),
             reads=["ones_row", "ldb_row"], writes=[PK[0]])
        P.do("vector", lambda e: e.tensor_copy(ldb[:, :], PSUM[0][:, :H]),
             reads=[PK[0]], writes=["ldb"])
        P.do("tensor", lambda e: e.matmul(PSUM[0][:, :n_cores], ones_row[:1, :128],
                                          sdel_sb[:1, :n_cores], start=True, stop=True),
             reads=["ones_row", "sdel_row"], writes=[PK[0]])
        P.do("vector", lambda e: e.tensor_copy(sdel_sb[:, :], PSUM[0][:, :n_cores]),
             reads=[PK[0]], writes=["sdel"])

        P.do("vector", lambda e: e.tensor_single_scalar(relu_d[:, :], it_a[:, :], 0.0, ALU.max),
             reads=["it_a"], writes=["relu_d"])
        P.do("vector", lambda e: e.tensor_single_scalar(it_a[:, :], it_a[:, :], 0.0, ALU.is_ge),
             reads=["it_a"], writes=["it_a"])
        for h in range(H):
            P.do("scalar", lambda e, h=h: e.activation(DTT[:, 128 * h:128 * h + 128], relu_d[:, :],
                                                       AF.Exp, bias=0.0, scale=ldb[:, h:h + 1]),
                 reads=["relu_d", "ldb"], writes=["DTT"])
        for h in range(H):
            P.do("vector", lambda e, h=h: e.tensor_tensor(DTT[:, 128 * h:128 * h + 128],
                                                          DTT[:, 128 * h:128 * h + 128],
                                                          it_a[:, :], ALU.mult),
                 reads=["DTT", "it_a"], writes=["DTT"])

        iot_col = sb("iot_col", [128, 1], F32)
        t127 = sb("t127", [128, 1], F32)
        P.do("gpsimd", lambda e: e.iota(iot_col[:, :], pattern=[[0, 1]], channel_multiplier=1,
                                        allow_small_or_imprecise_dtypes=True), writes=["iot_col"])
        P.do("vector", lambda e: e.tensor_scalar(t127[:, :], iot_col[:, :], -1.0, 127.0,
                                                 ALU.mult, ALU.add),
             reads=["iot_col"], writes=["t127"])
        for h in range(H):
            P.do("scalar", lambda e, h=h: e.activation(kdec1[:, h:h + 1], t127[:, :], AF.Exp,
                                                       bias=0.0, scale=ldb[:, h:h + 1]),
                 reads=["t127", "ldb"], writes=["kdec1"])

        iot_row = sb("iot_row", [128, 128], F32)
        P.do("gpsimd", lambda e: e.iota(iot_row[:, :], pattern=[[1, 128]], channel_multiplier=0,
                                        allow_small_or_imprecise_dtypes=True), writes=["iot_row"])
        P.do("vector", lambda e: e.tensor_single_scalar(iot_row[:, :], iot_row[:, :], 1.0, ALU.add),
             reads=["iot_row"], writes=["iot_row"])
        for h in range(H):
            P.do("scalar", lambda e, h=h: e.activation(qdecTfull[:, 128 * h:128 * h + 128],
                                                       iot_row[:, :], AF.Exp, bias=0.0,
                                                       scale=ldb[:, h:h + 1]),
                 reads=["iot_row", "ldb"], writes=["qdecT"])
        c128 = sb("c128", [128, 1], F32)
        P.do("vector", lambda e: e.memset(c128[:, :], 128.0), writes=["c128"])
        for h in range(H):
            P.do("scalar", lambda e, h=h: e.activation(cdbc[:, h:h + 1], c128[:, :], AF.Exp,
                                                       bias=0.0, scale=ldb[:, h:h + 1]),
                 reads=["c128", "ldb"], writes=["cdbc"])
        for h in range(H):
            P.do("vector", lambda e, h=h: e.tensor_single_scalar(
                cdfull[:, 128 * h:128 * h + 128], onesc[:, :], cdbc[:, h:h + 1], ALU.mult),
                reads=["onesc", "cdbc"], writes=["cdfull"])
        for k in range(n_cores):
            for h in range(H):
                P.do("scalar", lambda e, k=k, h=h: e.activation(
                    coefsh[:, k * H + h:k * H + h + 1], sdel_sb[:, k:k + 1], AF.Exp,
                    bias=0.0, scale=ldb[:, h:h + 1]),
                    reads=["sdel", "ldb"], writes=["coefsh"])

        P.do("vector", lambda e: e.memset(S_sb[:, :], 0.0), writes=["S"])

        for nm, ap_ in (("DTT", DTT), ("qdecT", qdecTfull),
                        ("coefsh", coefsh), ("cdbc", cdbc)):
            if tp := tap(nm, list(ap_.shape)):
                P.do("gpsimd", lambda e, tp=tp, ap_=ap_: e.dma_start(out=tp[:, :], in_=ap_[:, :]),
                     reads=[nm if nm != "cdbc" else "cdbc"], writes=["tapst"], dma=True)

        phase_close()

        P.scope("wprep")
        if SKIP_PREP:
            class _SkipCtx:
                pass

        phase_open()
        wld = [sb("wld%d" % i, [128, FF], F32) for i in range(2)]
        wq_t = sb("wq_t", [128, FF], F32)
        wq_a = sb("wq_a", [128, FF], F32)
        wq_o = sb("wq_o", [128, FF], BF16)
        racc = sb("racc", [128, 1], F32)
        rtile = sb("rtile", [128, 1], F32)
        sc1 = sb("sc1", [1, 4], F32)

        for wi, (w_in, wq_d, nrow) in enumerate(() if SKIP_PREP else ((w1_in, w1q_d, FF), (w2_in, w2q_d, D))):
            ntile = nrow // 128
            ncol = D if wi == 0 else FF
            P.do("vector", lambda e: e.memset(racc[:, :], 0.0), writes=["racc"])
            for t in range(ntile):
                buf = wld[t % 2]
                bk = "wld%d" % (t % 2)
                P.do("sync", lambda e, t=t, buf=buf, w_in=w_in, ncol=ncol: e.dma_start(
                    out=buf[:, :ncol], in_=w_in[128 * t:128 * t + 128, :]),
                    writes=[bk], dma=True)
                P.do("vector", lambda e, buf=buf, ncol=ncol: e.reduce_sum(
                    rtile[:, :], buf[:, :ncol], axis=AX, apply_absolute_value=True),
                    reads=[bk], writes=["rtile"])
                P.do("vector", lambda e: e.tensor_tensor(racc[:, :], racc[:, :], rtile[:, :],
                                                         ALU.add),
                     reads=["racc", "rtile"], writes=["racc"])
            P.do("tensor", lambda e: e.matmul(PSUM[0][:1, :1], racc[:, :1], onesc[:, :1],
                                              start=True, stop=True),
                 reads=["racc", "onesc"], writes=[PK[0]])
            P.do("vector", lambda e, ncol=ncol, nrow=nrow: e.tensor_scalar(
                sc1[:1, 0:1], PSUM[0][:1, :1], 1.0 / (nrow * ncol), 1e-8, ALU.mult, ALU.max),
                reads=[PK[0]], writes=["sc1"])
            P.do("vector", lambda e: e.reciprocal(sc1[:1, 1:2], sc1[:1, 0:1]),
                 reads=["sc1"], writes=["sc1"])
            P.do("vector", lambda e: e.tensor_copy(sc1[:1, 2:3], sc1[:1, 0:1]),
                 reads=["sc1"], writes=["sc1"])
            P.do("vector", lambda e: e.tensor_copy(sc1[:1, 0:1], sc1[:1, 1:2]),
                 reads=["sc1"], writes=["sc1"])
            P.do("vector", lambda e: e.tensor_copy(sc1[:1, 1:2], sc1[:1, 2:3]),
                 reads=["sc1"], writes=["sc1"])
            P.do("tensor", lambda e: e.matmul(PSUM[0][:, 0:2], ones_row[:1, :128], sc1[:1, 0:2],
                                              start=True, stop=True),
                 reads=["ones_row", "sc1"], writes=[PK[0]])
            P.do("vector", lambda e, wi=wi: e.tensor_copy(swb[:, 2 * wi:2 * wi + 2],
                                                          PSUM[0][:, 0:2]),
                 reads=[PK[0]], writes=["swb"])
            for t in range(ntile):
                buf = wld[t % 2]
                bk = "wld%d" % (t % 2)
                P.do("sync", lambda e, t=t, buf=buf, w_in=w_in, ncol=ncol: e.dma_start(
                    out=buf[:, :ncol], in_=w_in[128 * t:128 * t + 128, :]),
                    writes=[bk], dma=True)
                P.do("vector", lambda e, buf=buf, wi=wi, ncol=ncol: e.tensor_single_scalar(
                    wq_t[:, :ncol], buf[:, :ncol], swb[:, 2 * wi:2 * wi + 1], ALU.mult),
                    reads=[bk, "swb"], writes=["wq_t"])
                P.do("vector", lambda e, ncol=ncol: e.tensor_single_scalar(
                    wq_a[:, :ncol], wq_t[:, :ncol], 0.5, ALU.is_gt),
                    reads=["wq_t"], writes=["wq_a"])
                P.do("vector", lambda e, ncol=ncol: e.scalar_tensor_tensor(
                    wq_o[:, :ncol], wq_t[:, :ncol], -0.5, wq_a[:, :ncol],
                    ALU.is_lt, ALU.subtract),
                    reads=["wq_t", "wq_a"], writes=["wq_o"])
                P.do("gpsimd", lambda e, t=t, wq_d=wq_d, ncol=ncol: e.dma_start(
                    out=wq_d[128 * t:128 * t + 128, :], in_=wq_o[:, :ncol]),
                    reads=["wq_o"], writes=["wq_d%d" % wi], dma=True)
        if tp := tap("w1q", [256, D], BF16):
            P.do("sync", lambda e, tp=tp: e.dma_start(out=tp[:, :], in_=w1q_d[0:256, :]),
                 reads=["wq_d0"], writes=["tapst"], dma=True)
        phase_close()

        phase_open()
        P.scope("a1")
        kdecfull = sb("kdecfull", [128, D], F32)
        for h in range(H):
            P.do("vector", lambda e, h=h: e.tensor_single_scalar(
                kdecfull[:, 128 * h:128 * h + 128], onesc[:, :], kdec1[:, h:h + 1], ALU.mult),
                reads=["onesc", "kdec1"], writes=["kdecfull"])
        xa = sb("xa", [128, D], F32)
        xna = sb("xna", [128, D], F32)
        red8 = sb("red8", [128, 8], F32)
        xnTf = sb("xnTf", [128, D], F32)
        sq_scr = xnTf

        for j in range(n_chunks):
            P.do("sync", lambda e, j=j: e.dma_start(out=xa[:, :], in_=x_in[C * j:C * j + C, :]),
                 writes=["xa"], dma=True)
            P.do("vector", lambda e: e.reduce_sum(red8[:, 0:1], xa[:, :], axis=AX),
                 reads=["xa"], writes=["red_s"])
            P.do("scalar", lambda e: e.activation(sq_scr[:, :], xa[:, :], AF.Square,
                                                  accum_out=red8[:, 1:2]),
                 reads=["xa"], writes=["red_q", "xnTf"])
            P.do("vector", lambda e, j=j: e.tensor_single_scalar(mu_sb[:, j:j + 1], red8[:, 0:1],
                                                                 1.0 / D, ALU.mult),
                 reads=["red_s"], writes=["mu"])
            P.do("vector", lambda e, j=j: e.tensor_tensor(red8[:, 2:3], mu_sb[:, j:j + 1],
                                                          mu_sb[:, j:j + 1], ALU.mult),
                 reads=["mu"], writes=["red_m2"])
            P.do("vector", lambda e: e.tensor_scalar(red8[:, 3:4], red8[:, 1:2], 1.0 / D, None,
                                                     ALU.mult, ALU.bypass),
                 reads=["red_q"], writes=["red_v"])
            P.do("vector", lambda e: e.tensor_tensor(red8[:, 3:4], red8[:, 3:4], red8[:, 2:3],
                                                     ALU.subtract),
                 reads=["red_v", "red_m2"], writes=["red_v"])
            P.do("scalar", lambda e: e.activation(red8[:, 4:5], red8[:, 3:4], AF.Sqrt,
                                                  bias=epsv[:, :], scale=1.0),
                 reads=["red_v", "epsv"], writes=["red_sd"])
            P.do("vector", lambda e, j=j: e.reciprocal(rstd_sb[:, j:j + 1], red8[:, 4:5]),
                 reads=["red_sd"], writes=["rstd"])
            P.do("vector", lambda e, j=j: e.tensor_scalar(xna[:, :], xa[:, :], mu_sb[:, j:j + 1],
                                                          rstd_sb[:, j:j + 1], ALU.subtract,
                                                          ALU.mult),
                 reads=["xa", "mu", "rstd"], writes=["xna"])
            for b in range(NKD):
                P.do("tensor", lambda e, b=b: e.transpose(
                    PSUM[b // 4][:, 128 * (b % 4):128 * (b % 4) + 128],
                    xna[:, 128 * b:128 * b + 128], ident[:, :]),
                    reads=["xna", "ident"], writes=[PK[b // 4]])
            for q4 in range(4):
                P.do("vector", lambda e, q4=q4: e.tensor_copy(xnTf[:, 512 * q4:512 * q4 + 512],
                                                              PSUM[q4][:, :]),
                     reads=[PK[q4]], writes=["xnTf"])
            P.do("gpsimd", lambda e, j=j: e.dma_start(out=xnT_f_d[128 * j:128 * j + 128, :],
                                                      in_=xnTf[:, :]),
                 reads=["xnTf"], writes=["xnT_d"], dma=True)
            if j == 0 and (tp := tap("xn0", [128, D])):
                P.do("gpsimd", lambda e, tp=tp: e.dma_start(out=tp[:, :], in_=xna[:, :]),
                     reads=["xna"], writes=["tapst"], dma=True)
            if j == 0 and (tp := tap("xnT0", [128, D])):
                P.do("gpsimd", lambda e, tp=tp: e.dma_start(out=tp[:, :], in_=xnTf[:, :]),
                     reads=["xnTf"], writes=["tapst"], dma=True)

        P.barrier()

        P.scope("a2")
        TILE_A = min(8, n_chunks)
        n_tiles_A = n_chunks // TILE_A
        wkbuf = sb("wkbuf", [128, NKD * 512], F32)
        wvbuf = sb("wvbuf", [128, NKD * 512], F32)
        xnT_tile = sb("xnT_tile", [128, TILE_A * D], F32)
        kstage = sb("kstage", [128, 512], F32)
        ktil = sb("ktil", [128, 512], F32)
        vstage = sb("vstage", [128, 512], F32)
        kTstage = sb("kTstage", [128, 512], F32)
        kvstage = sb("kvstage", [128, 512], F32)

        def wslice_load(wbuf, wkey, w_in, eb, ew=512):
            for kk in range(NKD):
                P.do("sync", lambda e, kk=kk: e.dma_start(
                    out=wbuf[:, ew * kk:ew * kk + ew],
                    in_=w_in[128 * kk:128 * kk + 128, ew * eb:ew * eb + ew]),
                    writes=[wkey], dma=True)

        for ta in range(n_tiles_A):
            for j in range(TILE_A):
                cj = ta * TILE_A + j
                P.do("sync", lambda e, j=j, cj=cj: e.dma_start(
                    out=xnT_tile[:, D * j:D * j + D], in_=xnT_f_d[128 * cj:128 * cj + 128, :]),
                    reads=["xnT_d"], writes=["xnT_tile"], dma=True)
            if ta == 0 and (tp := tap("xnT_a2", [TILE_A * 128, D])):
                for jj in range(TILE_A):
                    P.do("gpsimd", lambda e, jj=jj, tp=tp: e.dma_start(
                        out=tp[128 * jj:128 * jj + 128, :],
                        in_=xnT_tile[:, D * jj:D * jj + D]),
                        reads=["xnT_tile"], writes=["tapst"], dma=True)
            for eb in range(0) if SKIP_A2_BODY else range(NEB):
                wslice_load(wkbuf, "wkbuf", Wk_in, eb)
                wslice_load(wvbuf, "wvbuf", Wv_in, eb)
                for j in range(0) if A2_ONLY_WLOADS else range(TILE_A):
                    cj = ta * TILE_A + j

                    def mk_mms(e, j=j, wbuf=wkbuf, bank=4):
                        for kk in range(NKD):
                            inst = e.matmul(PSUM[bank][:, :],
                                            xnT_tile[:, D * j + 128 * kk:D * j + 128 * kk + 128],
                                            wbuf[:, 512 * kk:512 * kk + 512],
                                            start=(kk == 0), stop=(kk == NKD - 1))
                        return inst
                    P.do("tensor", mk_mms, reads=["xnT_tile", "wkbuf"], writes=[PK[4]])
                    P.do("tensor", lambda e, j=j: mk_mms(e, j, wvbuf, 5),
                         reads=["xnT_tile", "wvbuf"], writes=[PK[5]])
                    P.do("scalar", lambda e: e.activation(kstage[:, :], PSUM[4][:, :], AF.Silu),
                         reads=[PK[4]], writes=["kstage"])
                    P.do("scalar", lambda e: e.activation(vstage[:, :], PSUM[5][:, :], AF.Silu),
                         reads=[PK[5]], writes=["vstage"])
                    P.do("gpsimd", lambda e, cj=cj, eb=eb: e.dma_start(
                        out=v_d[128 * cj:128 * cj + 128, 512 * eb:512 * eb + 512],
                        in_=vstage[:, :]),
                        reads=["vstage"], writes=["v_d"], dma=True)
                    P.do("vector", lambda e, eb=eb: e.tensor_tensor(
                        ktil[:, :], kstage[:, :], kdecfull[:, 512 * eb:512 * eb + 512], ALU.mult),
                        reads=["kstage", "kdecfull"], writes=["ktil"])
                    for hh in range(4):
                        P.do("tensor", lambda e, hh=hh: e.transpose(
                            PSUM[6][:, 128 * hh:128 * hh + 128],
                            kstage[:, 128 * hh:128 * hh + 128], ident[:, :]),
                            reads=["kstage", "ident"], writes=[PK[6]])
                    P.do("vector", lambda e: e.tensor_copy(kTstage[:, :], PSUM[6][:, :]),
                         reads=[PK[6]], writes=["kTstage"])
                    P.do("gpsimd", lambda e, cj=cj, eb=eb: e.dma_start(
                        out=kT_d[128 * cj:128 * cj + 128, 512 * eb:512 * eb + 512],
                        in_=kTstage[:, :]),
                        reads=["kTstage"], writes=["kT_d"], dma=True)

                    def kv_mms(e):
                        for hh in range(4):
                            inst = e.matmul(PSUM[7][:, 128 * hh:128 * hh + 128],
                                            ktil[:, 128 * hh:128 * hh + 128],
                                            vstage[:, 128 * hh:128 * hh + 128],
                                            start=True, stop=True)
                        return inst
                    P.do("tensor", kv_mms, reads=["ktil", "vstage"], writes=[PK[7]])
                    P.do("vector", lambda e, eb=eb: e.tensor_tensor(
                        S_sb[:, 512 * eb:512 * eb + 512], S_sb[:, 512 * eb:512 * eb + 512],
                        cdfull[:, 512 * eb:512 * eb + 512], ALU.mult),
                        reads=["S", "cdfull"], writes=["S"])
                    P.do("vector", lambda e, eb=eb: e.tensor_tensor(
                        S_sb[:, 512 * eb:512 * eb + 512], S_sb[:, 512 * eb:512 * eb + 512],
                        PSUM[7][:, :], ALU.add),
                        reads=["S", PK[7]], writes=["S"])
                    P.do("vector", lambda e: e.tensor_copy(kvstage[:, :], PSUM[7][:, :]),
                         reads=[PK[7]], writes=["kvstage"])
                    P.do("gpsimd", lambda e, cj=cj, eb=eb: e.dma_start(
                        out=kv_d[128 * cj:128 * cj + 128, 512 * eb:512 * eb + 512],
                        in_=kvstage[:, :]),
                        reads=["kvstage"], writes=["kv_d"], dma=True)
                    if DIAG_BARRIER:
                        P.barrier()

        if tp := tap("S_A", [128, H * HD]):
            P.do("gpsimd", lambda e, tp=tp: e.dma_start(out=tp[:, :], in_=S_sb[:, :]),
                 reads=["S"], writes=["tapst"], dma=True)
        if tp := tap("kT0", [128, D]):
            P.do("sync", lambda e, tp=tp: e.dma_start(out=tp[:, :], in_=kT_d[0:128, :]),
                 reads=["kT_d"], writes=["tapst"], dma=True)
        if tp := tap("v0", [128, D]):
            P.do("sync", lambda e, tp=tp: e.dma_start(out=tp[:, :], in_=v_d[0:128, :]),
                 reads=["v_d"], writes=["tapst"], dma=True)
        if tp := tap("kT_all", [n_chunks * 128, D]):
            P.do("sync", lambda e, tp=tp: e.dma_start(out=tp[:, :], in_=kT_d[:, :]),
                 reads=["kT_d"], writes=["tapst"], dma=True)
        if tp := tap("v_all", [n_chunks * 128, D]):
            P.do("sync", lambda e, tp=tp: e.dma_start(out=tp[:, :], in_=v_d[:, :]),
                 reads=["v_d"], writes=["tapst"], dma=True)
        if tp := tap("kv_all", [n_chunks * 128, H * HD]):
            P.do("sync", lambda e, tp=tp: e.dma_start(out=tp[:, :], in_=kv_d[:, :]),
                 reads=["kv_d"], writes=["tapst"], dma=True)
        if tp := tap("kv0", [128, H * HD]):
            P.do("sync", lambda e, tp=tp: e.dma_start(out=tp[:, :], in_=kv_d[0:128, :]),
                 reads=["kv_d"], writes=["tapst"], dma=True)

        phase_close()
        P.scope("coll")
        phase_open()
        P.do("gpsimd", lambda e: e.dma_start(out=cc_in_d[:, :], in_=S_sb[:, :]),
             reads=["S"], writes=["cc_in"], dma=True)
        P.do("gpsimd", lambda e: e.collective_compute(
            "AllGather", ALU.bypass, replica_groups=[list(range(n_cores))],
            ins=[cc_in_d[:, :]], outs=[cc_out_d[:, :]]),
            reads=["cc_in"], writes=["cc_out"])
        P.do("vector", lambda e: e.memset(S_sb[:, :], 0.0), reads=["S"], writes=["S"])
        Tk_buf = sb("Tk_buf", [128, H * HD], F32)
        for kcore in range(n_cores):
            P.do("sync", lambda e, kcore=kcore: e.dma_start(
                out=Tk_buf[:, :], in_=cc_out_d[128 * kcore:128 * kcore + 128, :]),
                reads=["cc_out"], writes=["Tk_buf"], dma=True)
            for h in range(H):
                P.do("vector", lambda e, kcore=kcore, h=h: e.scalar_tensor_tensor(
                    S_sb[:, 128 * h:128 * h + 128], Tk_buf[:, 128 * h:128 * h + 128],
                    coefsh[:, kcore * H + h:kcore * H + h + 1],
                    S_sb[:, 128 * h:128 * h + 128], ALU.mult, ALU.add),
                    reads=["Tk_buf", "coefsh", "S"], writes=["S"])
        if tp := tap("S_start", [128, H * HD]):
            P.do("gpsimd", lambda e, tp=tp: e.dma_start(out=tp[:, :], in_=S_sb[:, :]),
                 reads=["S"], writes=["tapst"], dma=True)

        phase_close()

        P.scope("pB")
        phase_open()
        TILE_B = min(4, n_chunks)
        n_tiles_B = n_chunks // TILE_B
        xnT_tile1 = sb("xnT_tile1", [128, TILE_B * D], F32)
        wqbuf = sb("wqbuf", [128, NKD * 512], F32)
        wg1buf = sb("wg1buf", [128, NKD * 128], F32)
        wg2buf = sb("wg2buf", [128, D], F32)
        g1tile = sb("g1tile", [128, TILE_B * 128], F32)

        qstage = sb("qstage", [128, 512], F32)
        qTs = sb("qTs", [128, 512], F32)
        qts = sb("qts", [128, 512], F32)
        kTr = sb("kTr", [128, 512], F32)
        vr = sb("vr", [128, 512], F32)
        kvr = sb("kvr", [128, 512], F32)
        scsb = sb("scsb", [128, 512], F32)
        sgst = sb("sgst", [128, 512], F32)
        ogst = sb("ogst", [128, 512], F32)
        ssq4 = sb("ssq4", [128, 4], F32)
        rinv4 = sb("rinv4", [128, 4], F32)
        ogTst = qts
        xq_xn = sb("xq_xn", [128, D], F32)
        xq_u = sb("xq_u", [128, 512], F32)
        xq_c = sb("xq_c", [128, 512], I32)
        xq_m = sb("xq_m", [128, 512], F32)
        xq_bf = sb("xq_bf", [128, 512], BF16)
        xqT_tile = sb("xqT_tile", [128, TILE_B * D], BF16)
        q1r = sb("q1r", [128, 8], F32)
        ub1 = sb("ub1", [128, 1], F32)
        w1qTbuf = [sb("w1qTbuf%d" % i, [128, NKD * 512], BF16) for i in range(2)]
        fstt = sb("fstt", [128, 512], F32)
        hstage = sb("hstage", [128, 512], F32)
        hsq = fstt
        b1slice = sb("b1slice", [128, 512], F32)
        hstat = sb("hstat", [128, 4 * TILE_B], F32)
        st4 = sb("st4", [128, 4], F32)
        b1row_s = sb("b1row_s", [1, 512], F32)
        for kk in range(NKD):
            P.do("sync", lambda e, kk=kk: e.dma_start(
                out=wg1buf[:, 128 * kk:128 * kk + 128],
                in_=Wg1_in[128 * kk:128 * kk + 128, :]),
                writes=["wg1buf"], dma=True)
        P.do("sync", lambda e: e.dma_start(out=wg2buf[:, :], in_=Wg2_in[:, :]),
             writes=["wg2buf"], dma=True)

        def quant_scalars(statcols, statkey, a_out_sb, a_col, a_key, sw_col):
            sc = statcols
            prefix = statkey
            P.do("vector", lambda e, sc=sc: e.tensor_single_scalar(
                q1r[:, 0:1], sc[0], 1.0 / sc[4], ALU.mult), reads=[prefix], writes=["q1r"])
            P.do("vector", lambda e, sc=sc: e.tensor_single_scalar(
                q1r[:, 1:2], sc[1], 1.0 / sc[4], ALU.mult), reads=[prefix], writes=["q1r"])
            P.do("vector", lambda e: e.tensor_tensor(q1r[:, 2:3], q1r[:, 0:1], q1r[:, 0:1],
                                                     ALU.mult), reads=["q1r"], writes=["q1r"])
            P.do("vector", lambda e: e.tensor_tensor(q1r[:, 1:2], q1r[:, 1:2], q1r[:, 2:3],
                                                     ALU.subtract), reads=["q1r"], writes=["q1r"])
            P.do("scalar", lambda e: e.activation(q1r[:, 2:3], q1r[:, 1:2], AF.Sqrt,
                                                  bias=epsv[:, :], scale=1.0),
                 reads=["q1r", "epsv"], writes=["q1r"])
            P.do("vector", lambda e: e.reciprocal(q1r[:, 3:4], q1r[:, 2:3]),
                 reads=["q1r"], writes=["q1r"])
            P.do("vector", lambda e, sc=sc: e.tensor_tensor(q1r[:, 4:5], sc[2], q1r[:, 0:1],
                                                            ALU.subtract),
                 reads=[prefix, "q1r"], writes=["q1r"])
            P.do("vector", lambda e, sc=sc: e.tensor_tensor(q1r[:, 5:6], q1r[:, 0:1], sc[3],
                                                            ALU.subtract),
                 reads=["q1r", prefix], writes=["q1r"])
            P.do("vector", lambda e: e.tensor_tensor(q1r[:, 4:5], q1r[:, 4:5], q1r[:, 5:6],
                                                     ALU.max), reads=["q1r"], writes=["q1r"])
            P.do("vector", lambda e: e.tensor_tensor(q1r[:, 4:5], q1r[:, 4:5], q1r[:, 3:4],
                                                     ALU.mult), reads=["q1r"], writes=["q1r"])
            P.do("vector", lambda e: e.tensor_single_scalar(q1r[:, 4:5], q1r[:, 4:5], EPS,
                                                            ALU.max), reads=["q1r"], writes=["q1r"])
            P.do("vector", lambda e: e.reciprocal(q1r[:, 5:6], q1r[:, 4:5]),
                 reads=["q1r"], writes=["q1r"])
            P.do("vector", lambda e: e.tensor_scalar(q1r[:, 6:7], q1r[:, 5:6], 127.0, None,
                                                     ALU.mult, ALU.bypass),
                 reads=["q1r"], writes=["q1r"])
            P.do("vector", lambda e: e.tensor_tensor(q1r[:, 6:7], q1r[:, 6:7], q1r[:, 3:4],
                                                     ALU.mult), reads=["q1r"], writes=["q1r"])
            P.do("vector", lambda e: e.tensor_tensor(ub1[:, :], q1r[:, 6:7], q1r[:, 0:1],
                                                     ALU.mult), reads=["q1r"], writes=["ub1"])
            P.do("vector", lambda e: e.tensor_scalar(ub1[:, :], ub1[:, :], -1.0, 16384.0,
                                                     ALU.mult, ALU.add),
                 reads=["ub1"], writes=["ub1"])
            P.do("vector", lambda e, sw_col=sw_col: e.tensor_tensor(
                q1r[:, 7:8], q1r[:, 4:5], swb[:, sw_col:sw_col + 1], ALU.mult),
                reads=["q1r", "swb"], writes=["q1r"])
            P.do("vector", lambda e, a_out_sb=a_out_sb, a_col=a_col, a_key=a_key: e.tensor_scalar(
                a_out_sb[:, a_col:a_col + 1], q1r[:, 7:8], -1.0 / 127.0, None,
                ALU.mult, ALU.bypass),
                reads=["q1r"], writes=[a_key])

        def emit_round(src_ap, srckey, dst_ap, dstkey, scale_ap, bias_ap, skeys, n_total):
            assert n_total == 512
            P.do("scalar", lambda e: e.activation(
                xq_u[:, :], src_ap[:, :], AF.Relu, bias=bias_ap, scale=scale_ap),
                reads=[srckey] + skeys, writes=["xq_u"])
            P.do("vector", lambda e: e.tensor_copy(xq_c[:, :], xq_u[:, :]),
                 reads=["xq_u"], writes=["xq_c"])
            P.do("vector", lambda e: e.scalar_tensor_tensor(
                xq_m[:, :], xq_u[:, :], -0.5, xq_c[:, :], ALU.add, ALU.is_gt),
                reads=["xq_u", "xq_c"], writes=["xq_m"])
            P.do("vector", lambda e: e.scalar_tensor_tensor(
                dst_ap[:, :], xq_m[:, :], -16384.0, xq_c[:, :], ALU.add, ALU.add),
                reads=["xq_m", "xq_c"], writes=[dstkey])

        for tb in range(n_tiles_B):
            for j in range(TILE_B):
                cj = tb * TILE_B + j
                P.do("sync", lambda e, j=j, cj=cj: e.dma_start(
                    out=xnT_tile1[:, D * j:D * j + D], in_=xnT_f_d[128 * cj:128 * cj + 128, :]),
                    reads=["xnT_d"], writes=["xnT_tile1"], dma=True)

                def g1_mms(e, j=j):
                    for kk in range(NKD):
                        inst = e.matmul(PSUM[5][:, :128],
                                        xnT_tile1[:, D * j + 128 * kk:D * j + 128 * kk + 128],
                                        wg1buf[:, 128 * kk:128 * kk + 128],
                                        start=(kk == 0), stop=(kk == NKD - 1))
                    return inst
                P.do("tensor", g1_mms, reads=["xnT_tile1", "wg1buf"], writes=[PK[5]])
                P.do("vector", lambda e: e.tensor_copy(scsb[:, :128], PSUM[5][:, :128]),
                     reads=[PK[5]], writes=["scsb"])
                P.do("tensor", lambda e, j=j: e.transpose(PSUM[5][:, :128], scsb[:, :128],
                                                          ident[:, :]),
                     reads=["scsb", "ident"], writes=[PK[5]])
                P.do("vector", lambda e, j=j: e.tensor_copy(g1tile[:, 128 * j:128 * j + 128],
                                                            PSUM[5][:, :128]),
                     reads=[PK[5]], writes=["g1tile"])
            for j in range(TILE_B):
                cj = tb * TILE_B + j
                P.do("sync", lambda e, cj=cj: e.dma_start(out=xq_xn[:, :],
                                                          in_=x_in[C * cj:C * cj + C, :]),
                     writes=["xq_xn"], dma=True)
                P.do("vector", lambda e, cj=cj: e.tensor_scalar(
                    xq_xn[:, :], xq_xn[:, :], mu_sb[:, cj:cj + 1], rstd_sb[:, cj:cj + 1],
                    ALU.subtract, ALU.mult),
                    reads=["xq_xn", "mu", "rstd"], writes=["xq_xn"])
                P.do("vector", lambda e: e.reduce_sum(st4[:, 0:1], xq_xn[:, :], axis=AX),
                     reads=["xq_xn"], writes=["st4"])
                P.do("scalar", lambda e, j=j: e.activation(
                    xqT_tile[:, D * j:D * j + D], xq_xn[:, :], AF.Square,
                    accum_out=st4[:, 1:2]),
                    reads=["xq_xn"], writes=["st4", "xqT_tile"])
                P.do("vector", lambda e: e.reduce_max(st4[:, 2:3], xq_xn[:, :], axis=AX),
                     reads=["xq_xn"], writes=["st4"])
                P.do("vector", lambda e: e.tensor_reduce(st4[:, 3:4], xq_xn[:, :], AX, ALU.min),
                     reads=["xq_xn"], writes=["st4"])
                quant_scalars((st4[:, 0:1], st4[:, 1:2], st4[:, 2:3], st4[:, 3:4], D),
                              "st4", a1_sb, cj, "a1m", 1)
                for quar in range(4):
                    emit_round(xq_xn[:, 512 * quar:512 * quar + 512], "xq_xn",
                               xq_bf, "xq_bf", q1r[:, 6:7], ub1[:, :],
                               ["q1r", "ub1"], 512)
                    for b4 in range(4):
                        b = quar * 4 + b4
                        P.do("tensor", lambda e, b4=b4, b=b: e.transpose(
                            PSUM[6 + (b // 4) % 2][:, 64 * (b % 4):64 * (b % 4) + 64].bitcast(BF16),
                            xq_bf[:, 128 * b4:128 * b4 + 128], identb[:, :]),
                            reads=["xq_bf", "identb"], writes=[PK[6 + (b // 4) % 2]])
                        if b % 4 == 3:
                            P.do("vector", lambda e, b=b, j=j: e.tensor_copy(
                                xqT_tile[:, D * j + 512 * (b // 4):D * j + 512 * (b // 4) + 512],
                                PSUM[6 + (b // 4) % 2][:, :256].bitcast(BF16)),
                                reads=[PK[6 + (b // 4) % 2]], writes=["xqT_tile"])
            for eb in range(NEB):
                wslice_load(wqbuf, "wqbuf", Wq_in, eb)
                for j in range(TILE_B):
                    cj = tb * TILE_B + j
                    P.do("sync", lambda e, cj=cj, eb=eb: e.dma_start(
                        out=kTr[:, :], in_=kT_d[128 * cj:128 * cj + 128, 512 * eb:512 * eb + 512]),
                        reads=["kT_d"], writes=["kTr"], dma=True)
                    P.do("sync", lambda e, cj=cj, eb=eb: e.dma_start(
                        out=vr[:, :], in_=v_d[128 * cj:128 * cj + 128, 512 * eb:512 * eb + 512]),
                        reads=["v_d"], writes=["vr"], dma=True)
                    P.do("sync", lambda e, cj=cj, eb=eb: e.dma_start(
                        out=kvr[:, :], in_=kv_d[128 * cj:128 * cj + 128, 512 * eb:512 * eb + 512]),
                        reads=["kv_d"], writes=["kvr"], dma=True)

                    def q_mms(e, j=j):
                        for kk in range(NKD):
                            inst = e.matmul(PSUM[2][:, :],
                                            xnT_tile1[:, D * j + 128 * kk:D * j + 128 * kk + 128],
                                            wqbuf[:, 512 * kk:512 * kk + 512],
                                            start=(kk == 0), stop=(kk == NKD - 1))
                        return inst
                    P.do("tensor", q_mms, reads=["xnT_tile1", "wqbuf"], writes=[PK[2]])
                    P.do("scalar", lambda e: e.activation(qstage[:, :], PSUM[2][:, :], AF.Silu),
                         reads=[PK[2]], writes=["qstage"])
                    for hh in range(4):
                        P.do("tensor", lambda e, hh=hh: e.transpose(
                            PSUM[3][:, 128 * hh:128 * hh + 128],
                            qstage[:, 128 * hh:128 * hh + 128], ident[:, :]),
                            reads=["qstage", "ident"], writes=[PK[3]])
                    P.do("vector", lambda e: e.tensor_copy(qTs[:, :], PSUM[3][:, :]),
                         reads=[PK[3]], writes=["qTs"])
                    P.do("vector", lambda e, eb=eb: e.tensor_tensor(
                        qts[:, :], qTs[:, :], qdecTfull[:, 512 * eb:512 * eb + 512], ALU.mult),
                        reads=["qTs", "qdecT"], writes=["qts"])

                    def sc_mms(e):
                        for hh in range(4):
                            inst = e.matmul(PSUM[3][:, 128 * hh:128 * hh + 128],
                                            kTr[:, 128 * hh:128 * hh + 128],
                                            qTs[:, 128 * hh:128 * hh + 128],
                                            start=True, stop=True)
                        return inst
                    P.do("tensor", sc_mms, reads=["kTr", "qTs"], writes=[PK[3]])
                    P.do("vector", lambda e, eb=eb: e.tensor_tensor(
                        scsb[:, :], PSUM[3][:, :], DTT[:, 512 * eb:512 * eb + 512], ALU.mult),
                        reads=[PK[3], "DTT"], writes=["scsb"])

                    def o_mms(e, eb=eb):
                        for hh in range(4):
                            e.matmul(PSUM[4][:, 128 * hh:128 * hh + 128],
                                     scsb[:, 128 * hh:128 * hh + 128],
                                     vr[:, 128 * hh:128 * hh + 128],
                                     start=True, stop=False)
                            inst = e.matmul(PSUM[4][:, 128 * hh:128 * hh + 128],
                                            qts[:, 128 * hh:128 * hh + 128],
                                            S_sb[:, 128 * (4 * eb + hh):128 * (4 * eb + hh) + 128],
                                            start=False, stop=True)
                        return inst
                    P.do("tensor", o_mms, reads=["scsb", "vr", "qts", "S"], writes=[PK[4]])
                    P.do("vector", lambda e, eb=eb: e.tensor_tensor(
                        S_sb[:, 512 * eb:512 * eb + 512], S_sb[:, 512 * eb:512 * eb + 512],
                        cdfull[:, 512 * eb:512 * eb + 512], ALU.mult),
                        reads=["S", "cdfull"], writes=["S"])
                    P.do("vector", lambda e, eb=eb: e.tensor_tensor(
                        S_sb[:, 512 * eb:512 * eb + 512], S_sb[:, 512 * eb:512 * eb + 512],
                        kvr[:, :], ALU.add),
                        reads=["S", "kvr"], writes=["S"])
                    P.do("tensor", lambda e, j=j, eb=eb: e.matmul(
                        PSUM[5][:, :], g1tile[:, 128 * j:128 * j + 128],
                        wg2buf[:, 512 * eb:512 * eb + 512], start=True, stop=True),
                        reads=["g1tile", "wg2buf"], writes=[PK[5]])
                    P.do("scalar", lambda e: e.activation(sgst[:, :], PSUM[5][:, :], AF.Sigmoid),
                         reads=[PK[5]], writes=["sgst"])
                    P.do("vector", lambda e: e.tensor_tensor(ogst[:, :], PSUM[4][:, :],
                                                             sgst[:, :], ALU.mult),
                         reads=[PK[4], "sgst"], writes=["ogst"])
                    for hh in range(4):
                        P.do("scalar", lambda e, hh=hh: e.activation(
                            qstage[:, 128 * hh:128 * hh + 128], ogst[:, 128 * hh:128 * hh + 128],
                            AF.Square, accum_out=ssq4[:, hh:hh + 1]),
                            reads=["ogst"], writes=["qstage", "ssq4"])
                    P.do("scalar", lambda e: e.activation(ssq4[:, :], ssq4[:, :], AF.Sqrt,
                                                          bias=epsv[:, :], scale=1.0 / HD),
                         reads=["ssq4", "epsv"], writes=["ssq4"])
                    P.do("vector", lambda e: e.reciprocal(rinv4[:, :], ssq4[:, :]),
                         reads=["ssq4"], writes=["rinv4"])
                    for hh in range(4):
                        P.do("vector", lambda e, hh=hh: e.tensor_single_scalar(
                            ogst[:, 128 * hh:128 * hh + 128], ogst[:, 128 * hh:128 * hh + 128],
                            rinv4[:, hh:hh + 1], ALU.mult),
                            reads=["ogst", "rinv4"], writes=["ogst"])
                    for hh in range(4):
                        P.do("tensor", lambda e, hh=hh: e.transpose(
                            PSUM[4][:, 128 * hh:128 * hh + 128],
                            ogst[:, 128 * hh:128 * hh + 128], ident[:, :]),
                            reads=["ogst", "ident"], writes=[PK[4]])
                    P.do("vector", lambda e: e.tensor_copy(ogTst[:, :], PSUM[4][:, :]),
                         reads=[PK[4]], writes=["qts"])
                    P.do("gpsimd", lambda e, cj=cj, eb=eb: e.dma_start(
                        out=ogT_f_d[128 * cj:128 * cj + 128, 512 * eb:512 * eb + 512],
                        in_=ogTst[:, :]),
                        reads=["qts"], writes=["ogT_d"], dma=True)
                for fb in range(4 * eb, 4 * eb + 4):
                    w1b = w1qTbuf[fb % 2]
                    w1k = "w1qTbuf%d" % (fb % 2)
                    for kk in range(NKD):
                        P.do("sync", lambda e, fb=fb, kk=kk, w1b=w1b: e.dma_start_transpose(
                            w1b[:, 512 * kk:512 * kk + 512],
                            w1q_d[512 * fb:512 * fb + 512, 128 * kk:128 * kk + 128]),
                            reads=["wq_d0"], writes=[w1k], dma=True)
                    P.do("sync", lambda e, fb=fb: e.dma_start(
                        out=b1row_s[:, :], in_=b1_in[:, 512 * fb:512 * fb + 512]),
                        writes=["b1row"], dma=True)
                    P.do("tensor", lambda e, fb=fb: e.matmul(
                        PSUM[5][:, :], ones_row[:1, :128], b1row_s[:1, :],
                        start=True, stop=True),
                        reads=["ones_row", "b1row"], writes=[PK[5]])
                    P.do("vector", lambda e: e.tensor_copy(b1slice[:, :], PSUM[5][:, :]),
                         reads=[PK[5]], writes=["b1slice"])
                    for j in range(TILE_B):
                        cj = tb * TILE_B + j
                        fpar = 6 + (fb * TILE_B + j) % 2

                        def f1_mms(e, j=j, w1b=w1b, fpar=fpar):
                            for kk in range(NKD):
                                inst = e.matmul(PSUM[fpar][:, :],
                                                xqT_tile[:, D * j + 128 * kk:D * j + 128 * kk + 128],
                                                w1b[:, 512 * kk:512 * kk + 512],
                                                start=(kk == 0), stop=(kk == NKD - 1))
                            return inst
                        P.do("tensor", f1_mms, reads=["xqT_tile", w1k], writes=[PK[fpar]])
                        P.do("vector", lambda e, cj=cj, fpar=fpar: e.scalar_tensor_tensor(
                            fstt[:, :], PSUM[fpar][:, :], a1_sb[:, cj:cj + 1], b1slice[:, :],
                            ALU.mult, ALU.add),
                            reads=[PK[fpar], "a1m", "b1slice"], writes=["fstt"])
                        P.do("scalar", lambda e: e.activation(hstage[:, :], fstt[:, :], AF.Gelu,
                                                              accum_out=st4[:, 0:1]),
                             reads=["fstt"], writes=["hstage", "st4"])
                        P.do("scalar", lambda e: e.activation(hsq[:, :], hstage[:, :], AF.Square,
                                                              accum_out=st4[:, 1:2]),
                             reads=["hstage"], writes=["fstt", "st4"])
                        P.do("vector", lambda e: e.reduce_max(st4[:, 2:3], hstage[:, :], axis=AX),
                             reads=["hstage"], writes=["st4"])
                        P.do("vector", lambda e: e.tensor_reduce(st4[:, 3:4], hstage[:, :], AX,
                                                                 ALU.min),
                             reads=["hstage"], writes=["st4"])
                        if fb == 0:
                            P.do("vector", lambda e, j=j: e.tensor_copy(
                                hstat[:, 4 * j:4 * j + 4], st4[:, :]),
                                reads=["st4"], writes=["hstat"])
                        else:
                            for si, alu in ((0, ALU.add), (1, ALU.add), (2, ALU.max), (3, ALU.min)):
                                P.do("vector", lambda e, j=j, si=si, alu=alu: e.tensor_tensor(
                                    hstat[:, 4 * j + si:4 * j + si + 1],
                                    hstat[:, 4 * j + si:4 * j + si + 1], st4[:, si:si + 1], alu),
                                    reads=["hstat", "st4"], writes=["hstat"])
                        P.do("gpsimd", lambda e, cj=cj, fb=fb: e.dma_start(
                            out=h_d[128 * cj:128 * cj + 128, 512 * fb:512 * fb + 512],
                            in_=hstage[:, :]),
                            reads=["hstage"], writes=["h_d"], dma=True)
            for j in range(TILE_B):
                cj = tb * TILE_B + j
                quant_scalars((hstat[:, 4 * j:4 * j + 1], hstat[:, 4 * j + 1:4 * j + 2],
                               hstat[:, 4 * j + 2:4 * j + 3], hstat[:, 4 * j + 3:4 * j + 4],
                               FF), "hstat", a2_sb, cj, "a2m", 3)
                P.do("vector", lambda e, cj=cj: e.tensor_copy(aq2_sb[:, cj:cj + 1],
                                                              q1r[:, 6:7]),
                     reads=["q1r"], writes=["aq2"])
                P.do("vector", lambda e, cj=cj: e.tensor_copy(ub2_sb[:, cj:cj + 1], ub1[:, :]),
                     reads=["ub1"], writes=["ub2"])
        phase_close()

        P.scope("p2aii")
        phase_open()
        TILE_F2 = min(2, n_chunks)
        n_tiles_f2 = n_chunks // TILE_F2
        h_r = sb("h_r", [128, D], F32)
        h_u = sb("h_u", [128, D], F32)
        h_c = sb("h_c", [128, D], I32)
        h_dd = sb("h_dd", [128, D], F32)
        h_rr = sb("h_rr", [128, D], F32)
        h_bf = sb("h_bf", [128, D], F32)
        hqT_tile = sb("hqT_tile", [128, TILE_F2 * NKF * 128], BF16)
        w2qTbuf = [sb("w2qTbuf%d" % i, [128, D], BF16) for i in range(2)]
        b2s4 = sb("b2s4", [128, 4 * 512], F32)
        g2stt = sb("g2stt", [128, 512], F32)
        gstage = sb("gstage", [128, 512], F32)
        b2row = sb("b2row", [1, D], F32)
        P.do("sync", lambda e: e.dma_start(out=b2row[:, :], in_=b2_in[:, :]),
             writes=["b2row"], dma=True)

        for db in range(4):
            P.do("tensor", lambda e, db=db: e.matmul(
                PSUM[0][:, :], ones_row[:1, :128], b2row[:1, 512 * db:512 * db + 512],
                start=True, stop=True),
                reads=["ones_row", "b2row"], writes=[PK[0]])
            P.do("vector", lambda e, db=db: e.tensor_copy(b2s4[:, 512 * db:512 * db + 512],
                                                          PSUM[0][:, :]),
                 reads=[PK[0]], writes=["b2s4"])

        for tf in range(0) if SKIP_LATE else range(n_tiles_f2):
            for j in range(TILE_F2):
                cj = tf * TILE_F2 + j
                for ss in range(FF // D):
                    P.do("sync", lambda e, cj=cj, ss=ss: e.dma_start(
                        out=h_r[:, :], in_=h_d[128 * cj:128 * cj + 128, D * ss:D * ss + D]),
                        reads=["h_d"], writes=["h_r"], dma=True)
                    P.do("scalar", lambda e, cj=cj: e.activation(
                        h_u[:, :], h_r[:, :], AF.Relu, bias=ub2_sb[:, cj:cj + 1],
                        scale=aq2_sb[:, cj:cj + 1]),
                        reads=["h_r", "ub2", "aq2"], writes=["h_u"])
                    P.do("vector", lambda e: e.tensor_copy(h_c[:, :], h_u[:, :]),
                         reads=["h_u"], writes=["h_c"])
                    P.do("vector", lambda e: e.tensor_tensor(h_dd[:, :], h_u[:, :], h_c[:, :],
                                                             ALU.subtract),
                         reads=["h_u", "h_c"], writes=["h_dd"])
                    P.do("vector", lambda e: e.scalar_tensor_tensor(
                        h_rr[:, :], h_dd[:, :], 0.5, h_c[:, :], ALU.is_gt, ALU.add),
                        reads=["h_dd", "h_c"], writes=["h_rr"])
                    P.do("vector", lambda e: e.tensor_single_scalar(
                        h_bf[:, :], h_rr[:, :], -16384.0, ALU.add),
                        reads=["h_rr"], writes=["h_bf"])
                    for b in range(NKD):
                        P.do("tensor", lambda e, b=b: e.transpose(
                            PSUM[b // 4][:, 128 * (b % 4):128 * (b % 4) + 128],
                            h_bf[:, 128 * b:128 * b + 128], ident[:, :]),
                            reads=["h_bf", "ident"], writes=[PK[b // 4]])
                    for q4 in range(4):
                        P.do("vector", lambda e, q4=q4, j=j, ss=ss: e.tensor_copy(
                            hqT_tile[:, (j * NKF + ss * 16 + q4 * 4) * 128:
                                     (j * NKF + ss * 16 + q4 * 4 + 4) * 128],
                            PSUM[q4][:, :]),
                            reads=[PK[q4]], writes=["hqT_tile"])
            for ff in range(NKF):
                buf = w2qTbuf[ff % 2]
                bk = "w2qT%d" % (ff % 2)
                P.do("scalar", lambda e, ff=ff, buf=buf: e.dma_start_transpose(
                    buf[:, :], w2q_d[:, 128 * ff:128 * ff + 128]),
                    reads=["wq_d1"], writes=[bk], dma=True)

                def f2_mms(e, ff=ff, buf=buf):
                    for j in range(TILE_F2):
                        for db in range(4):
                            inst = e.matmul(
                                PSUM[4 * j + db][:, :],
                                hqT_tile[:, (j * NKF + ff) * 128:(j * NKF + ff) * 128 + 128],
                                buf[:, 512 * db:512 * db + 512],
                                start=(ff == 0), stop=(ff == NKF - 1),
                                skip_group_check=True)
                    return inst
                P.do("tensor", f2_mms, reads=["hqT_tile", bk],
                     writes=[PK[i] for i in range(8)])
            for j in range(TILE_F2):
                cj = tf * TILE_F2 + j
                for db in range(4):
                    P.do("vector", lambda e, cj=cj, j=j, db=db: e.scalar_tensor_tensor(
                        g2stt[:, :], PSUM[4 * j + db][:, :], a2_sb[:, cj:cj + 1],
                        b2s4[:, 512 * db:512 * db + 512], ALU.mult, ALU.add),
                        reads=[PK[4 * j + db], "a2m", "b2s4"], writes=["g2stt"])
                    P.do("scalar", lambda e: e.activation(gstage[:, :], g2stt[:, :], AF.Silu),
                         reads=["g2stt"], writes=["gstage"])
                    P.do("gpsimd", lambda e, cj=cj, db=db: e.dma_start(
                        out=g_d[128 * cj:128 * cj + 128, 512 * db:512 * db + 512],
                        in_=gstage[:, :]),
                        reads=["gstage"], writes=["g_d"], dma=True)
        if tp := tap("g0", [128, D]):
            P.do("sync", lambda e, tp=tp: e.dma_start(out=tp[:, :], in_=g_d[0:128, :]),
                 reads=["g_d"], writes=["tapst"], dma=True)
        if tp := tap("g_all", [n_chunks * 128, D]):
            P.do("sync", lambda e, tp=tp: e.dma_start(out=tp[:, :], in_=g_d[:, :]),
                 reads=["g_d"], writes=["tapst"], dma=True)
        if tp := tap("h_all", [n_chunks * 128, FF]):
            P.do("sync", lambda e, tp=tp: e.dma_start(out=tp[:, :], in_=h_d[:, :]),
                 reads=["h_d"], writes=["tapst"], dma=True)
        phase_close()

        P.scope("p2b")
        phase_open()
        wobuf = sb("wobuf", [128, NKD * D], F32)
        nwcol = sb("nwcol", [128, NKD], F32)
        nwt16 = sb("nwt16", [16, 128], F32)
        ogTr = sb("ogTr", [128, D], F32)
        gr = sb("gr", [128, D], F32)
        xr2 = sb("xr2", [128, D], F32)
        ystage = sb("ystage", [128, D], F32)

        P.do("sync", lambda e: e.dma_start(out=nwt16[:, :],
                                           in_=nw_in[:, :].rearrange("o (b c) -> (o b) c", b=16)),
             writes=["nwt16"], dma=True)
        P.do("tensor", lambda e: e.transpose(PSUM[0][:, :16], nwt16[:, :], ident[:16, :16]),
             reads=["nwt16", "ident"], writes=[PK[0]])
        P.do("vector", lambda e: e.tensor_copy(nwcol[:, :], PSUM[0][:, :16]),
             reads=[PK[0]], writes=["nwcol"])
        for kk in range(NKD):
            P.do("sync", lambda e, kk=kk: e.dma_start(
                out=wobuf[:, D * kk:D * kk + D],
                in_=Wo_in[128 * kk:128 * kk + 128, :]),
                writes=["wobuf"], dma=True)
        for kk in range(NKD):
            P.do("vector", lambda e, kk=kk: e.tensor_single_scalar(
                wobuf[:, D * kk:D * kk + D], wobuf[:, D * kk:D * kk + D],
                nwcol[:, kk:kk + 1], ALU.mult),
                reads=["wobuf", "nwcol"], writes=["wobuf"])

        for cj in range(0) if SKIP_LATE else range(n_chunks):
            P.do("sync", lambda e, cj=cj: e.dma_start(out=ogTr[:, :],
                                                      in_=ogT_f_d[128 * cj:128 * cj + 128, :]),
                 reads=["ogT_d"], writes=["ogTr"], dma=True)
            P.do("sync", lambda e, cj=cj: e.dma_start(out=gr[:, :],
                                                      in_=g_d[128 * cj:128 * cj + 128, :]),
                 reads=["g_d"], writes=["gr"], dma=True)
            P.do("sync", lambda e, cj=cj: e.dma_start(out=xr2[:, :],
                                                      in_=x_in[C * cj:C * cj + C, :]),
                 writes=["xr2"], dma=True)
            for eb in range(NEB):
                def wo_mms(e, eb=eb):
                    for kk in range(NKD):
                        inst = e.matmul(PSUM[4 + (eb % 2)][:, :],
                                        ogTr[:, 128 * kk:128 * kk + 128],
                                        wobuf[:, D * kk + 512 * eb:D * kk + 512 * eb + 512],
                                        start=(kk == 0), stop=(kk == NKD - 1))
                    return inst
                P.do("tensor", wo_mms, reads=["ogTr", "wobuf"], writes=[PK[4 + (eb % 2)]])
                P.do("vector", lambda e, eb=eb: e.tensor_tensor(
                    ystage[:, 512 * eb:512 * eb + 512], PSUM[4 + (eb % 2)][:, :],
                    gr[:, 512 * eb:512 * eb + 512], ALU.mult),
                    reads=[PK[4 + (eb % 2)], "gr"], writes=["ystage"])
            P.do("vector", lambda e: e.tensor_tensor(ystage[:, :], ystage[:, :], xr2[:, :],
                                                     ALU.add),
                 reads=["ystage", "xr2"], writes=["ystage"])
            P.do("gpsimd", lambda e, cj=cj: e.dma_start(out=y_out[C * cj:C * cj + C, :],
                                                        in_=ystage[:, :]),
                 reads=["ystage"], writes=["y_d"], dma=True)
        phase_close()

        with nc.Block() as block:
            P.finalize_and_replay(block)

    return nc, tap_outs


N_CORES = 8
SPANS = 4
N_CH = 32
T_SPAN = N_CH * C

_prog_cache = {}


def _get_prog():
    if "nc" not in _prog_cache:
        nc, _ = build(n_chunks=N_CH, n_cores=N_CORES, taps=(), mm="f32")
        _prog_cache["nc"] = nc
    return _prog_cache["nc"]


def kernel(L0_attn_x, L0_attn_log_decay, L0_attn_Wq, L0_attn_Wk, L0_attn_Wv,
           L0_attn_Wo, L0_attn_Wg1, L0_attn_Wg2, L0_attn_norm_w,
           L0_ffn_w1, L0_ffn_b1, L0_ffn_w2, L0_ffn_b2):
    x = np.ascontiguousarray(np.asarray(L0_attn_x, np.float32))
    B, N, _ = x.shape
    ld = np.asarray(L0_attn_log_decay, np.float32).reshape(1, H)
    base = {
        "log_decay": ld,
        "Wq": np.ascontiguousarray(np.asarray(L0_attn_Wq, np.float32)),
        "Wk": np.ascontiguousarray(np.asarray(L0_attn_Wk, np.float32)),
        "Wv": np.ascontiguousarray(np.asarray(L0_attn_Wv, np.float32)),
        "Wo": np.ascontiguousarray(np.asarray(L0_attn_Wo, np.float32)),
        "Wg1": np.ascontiguousarray(np.asarray(L0_attn_Wg1, np.float32)),
        "Wg2": np.ascontiguousarray(np.asarray(L0_attn_Wg2, np.float32)),
        "norm_w": np.asarray(L0_attn_norm_w, np.float32).reshape(1, D),
        "w1": np.ascontiguousarray(np.asarray(L0_ffn_w1, np.float32)),
        "b1": np.asarray(L0_ffn_b1, np.float32).reshape(1, FF),
        "w2": np.ascontiguousarray(np.asarray(L0_ffn_w2, np.float32)),
        "b2": np.asarray(L0_ffn_b2, np.float32).reshape(1, D),
    }
    nc = _get_prog()
    in_maps = []
    for core in range(N_CORES):
        b, p = divmod(core, SPANS)
        sdel = np.full((1, N_CORES), HUGE_DELTA, np.float32)
        for kk in range(p):
            sdel[0, b * SPANS + kk] = T_SPAN * (p - 1 - kk)
        im = dict(base)
        im["x"] = np.ascontiguousarray(x[b, p * T_SPAN:(p + 1) * T_SPAN])
        im["span_deltas"] = sdel
        in_maps.append(im)
    res = run_bass_kernel_spmd(nc, in_maps, core_ids=list(range(N_CORES)))
    y = np.stack([np.concatenate([res.results[b * SPANS + p]["y"] for p in range(SPANS)], 0)
                  for b in range(B)], 0)
    return y
